# version 12
# baseline (speedup 1.0000x reference)
import numpy as np
import ml_dtypes
import concourse.bass as bass
import concourse.mybir as mybir
from concourse.alu_op_type import AluOpType as OP
from concourse.bass_utils import run_bass_kernel_spmd

F32 = mybir.dt.float32
BF16 = mybir.dt.bfloat16
I32 = mybir.dt.int32
AF = mybir.ActivationFunctionType

D = 2048
T = 2048
NK = 16
TC = 512
NTC = 4
EPS = 1e-5
NIB = 64

_CHAINED = {
    "tensor_tensor", "tensor_scalar", "tensor_single_scalar",
    "scalar_tensor_tensor", "tensor_copy", "tensor_reduce", "reduce_max",
    "reduce_sum", "reciprocal", "memset", "memzero", "iota", "activation",
    "copy", "mul", "add", "square", "sqrt", "sign", "tensor_tensor_scan",
    "select", "tensor_add", "tensor_mul", "tensor_sub", "tensor_max",
    "tensor_relu", "max_index", "max_with_indices", "bn_stats", "bn_aggr",
    "copy_predicated", "stream_shuffle", "tensor_scalar_add",
    "tensor_scalar_mul", "tensor_scalar_max", "tensor_scalar_min",
    "tensor_scalar_sub",
}


class Sem:
    def __init__(self, nc, name):
        self.h = nc.alloc_semaphore(name)
        self.n = 0


class Chained:

    def __init__(self, eng, sem):
        self._e = eng
        self._s = sem

    def __getattr__(self, name):
        attr = getattr(self._e, name)
        if name not in _CHAINED:
            return attr
        sem = self._s
        eng = self._e

        def wrapped(*a, **k):
            if sem.n > 0:
                eng.wait_ge(sem.h, sem.n)
            instr = attr(*a, **k)
            instr.then_inc(sem.h, 1)
            sem.n += 1
            return instr

        return wrapped

    @property
    def tok(self):
        return self._s.n

    def wait(self, sem, val):
        if val > 0:
            self._e.wait_ge(sem.h, val)


class P:

    def __init__(self, nc):
        self.nc = nc
        self.t = nc.tensor
        self.mm = Sem(nc, "mm")
        self.ev = Sem(nc, "ev")
        self.sc = Sem(nc, "sc")
        self.gp = Sem(nc, "gp")
        self.v = Chained(nc.vector, self.ev)
        self.s = Chained(nc.scalar, self.sc)
        self.g = Chained(nc.gpsimd, self.gp)
        self.q = nc.sync
        self.xin = Sem(nc, "xin")
        self.xint = [Sem(nc, f"xint{i}") for i in range(4)]
        self.wgrp = Sem(nc, "wgrp")
        self.hld = Sem(nc, "hld")
        self.xq = Sem(nc, "xq")
        self.wsl = [Sem(nc, f"wsl{i}") for i in range(6)]
        self.rsl = [Sem(nc, f"rsl{i}") for i in range(4)]
        self.ssl = [Sem(nc, f"ssl{i}") for i in range(4)]
        self.hsl = [Sem(nc, f"hsl{i}") for i in range(2)]
        self.ps = [nc.alloc_psum_tensor(f"ps{i}", [128, 512], F32) for i in range(8)]
        A = nc.alloc_sbuf_tensor
        self.XBF = A("XBF", [128, NK * T], BF16)
        self.BIGB = A("BIGB", [128, 32768], BF16)
        self.WBUF = A("WBUF", [128, 8192], BF16)
        self.TMPF = A("TMPF", [128, 2048], F32)
        self.RESG = A("RESG", [128, 2048], F32)
        self.HEV = A("HEV", [128, 4096], BF16)
        self.MROW = A("MROW", [1, 2048], F32)
        self.RROW = A("RROW", [1, 2048], F32)
        self.NROW = A("NROW", [1, 2048], F32)
        self.RARE = A("RARE", [1, 2048], F32)
        self.SM = A("SM", [128, 1024], F32)
        self.ONEB = A("ONEB", [128, 1], BF16)
        self.ONE1 = A("ONE1", [1, 128], F32)
        self.ONEF = A("ONEF", [128, 1], F32)
        self.ONE8 = A("ONE8", [128, 8], F32)
        self.EPSC = A("EPSC", [128, 1], F32)
        self.HALFC = A("HALFC", [128, 1], F32)
        self.EHN = A("EHN", [32, 512], F32)
        self.ID128 = A("ID128", [128, 128], F32)
        self.CON = A("CON", [128, 128], F32)
        self.ESUM = self.CON[0:64, 52:60]
        self.E2F = self.CON[0:8, 64:128]
        self.XBF_F = self.XBF.ap().bitcast(F32)
        self.BIG_F = self.BIGB.ap().bitcast(F32)

    def _mm(self, last_instr):
        last_instr.then_inc(self.mm.h, 1)
        self.mm.n += 1
        return self.mm.n

    def _dma(self, eng, sem, out, in_):
        sem.n += 16
        eng.dma_start(out=out, in_=in_).then_inc(sem.h, 16)
        return sem.n

    def emit_consts(self):
        g = self.g
        g.memset(self.ONEB[:], 1.0)
        g.memset(self.ONE1[:], 1.0)
        g.memset(self.ONEF[:], 1.0)
        g.memset(self.ONE8[:], 1.0)
        g.memset(self.EPSC[:], EPS)
        g.memset(self.HALFC[:], 0.5)
        it32 = self.SM.ap().bitcast(I32)
        g.iota(it32[0:32, 0:512], [[0, 1], [0, 1], [-1, 32], [0, 16]],
               base=0, channel_multiplier=1)
        g.iota(it32[:, 512:640], [[-1, 128]], base=0, channel_multiplier=1)
        g.iota(it32[0:8, 648:712], [[0, 1], [0, 1], [-1, 8], [0, 8]],
               base=0, channel_multiplier=1)
        g.tensor_single_scalar(self.EHN[:], it32[0:32, 0:512], 0.0, OP.is_equal)
        g.tensor_single_scalar(self.ID128[:], it32[:, 512:640], 0.0, OP.is_equal)
        g.tensor_single_scalar(self.E2F, it32[0:8, 648:712], 0.0, OP.is_equal)
        self.con_done = self.gp.n
        self.t.wait_ge(self.gp.h, self.con_done)
        mm_es = self._mm(self.t.transpose(self.ps[0][0:64, 0:8], self.E2F,
                                          self.ID128[0:8, 0:8]))
        self.v.wait(self.mm, mm_es)
        self.v.tensor_copy(self.ESUM, self.ps[0][0:64, 0:8])
        self.esum_ev = self.v.tok
        self.t.wait_ge(self.ev.h, self.esum_ev)
        self.q.wait_ge(self.gp.h, self.con_done)
        self.s.wait(self.gp, self.con_done)

    def emit_ln(self, res_src, need_rn=False):
        t, v, s, g = self.t, self.v, self.s, self.g
        XBF = self.XBF
        ONES = self.HEV[:, 2048:2176]
        xsq = self.HEV.ap()
        NRS = self.BIG_F[0:8, 11008:13056]
        xv = XBF.ap().rearrange("p (a c) -> p a c", a=NK)
        self.ln_ev_tc = [0] * NTC
        g.memset(ONES, 1.0)
        gp_ones = self.gp.n
        t.wait_ge(self.gp.h, gp_ones)
        ld_tok = []
        for tc in range(NTC):
            cs = slice(tc * TC, (tc + 1) * TC)
            for k0 in (0, 4, 8, 12):
                self._dma(g, self.xint[tc], xv[:, k0:k0 + 4, cs],
                          res_src[k0 * 128:(k0 + 4) * 128, cs].rearrange(
                              "(a p) c -> p a c", p=128))
            ld_tok.append(self.xint[tc].n)
        sq_tok = []
        prev_ev = [0, 0]
        ev_norm_prev = 0
        for tc in range(NTC):
            cs = slice(tc * TC, (tc + 1) * TC)
            p = tc % 2
            ps_a, ps_b = self.ps[4 + p], self.ps[6 + p]
            MT = self.TMPF[:, p * TC:(p + 1) * TC]
            VT = self.TMPF[:, 1024 + p * TC: 1024 + (p + 1) * TC]
            s.wait(self.xint[tc], ld_tok[tc])
            t.wait_ge(self.xint[tc].h, ld_tok[tc])
            if tc >= 2:
                t.wait_ge(self.ev.h, prev_ev[p])
            for k in range(NK):
                it = tc * NK + k
                sl = xsq[:, (it % 2) * TC:(it % 2 + 1) * TC]
                xsl = XBF[:, k * T + tc * TC: k * T + (tc + 1) * TC]
                if it >= 2:
                    s.wait(self.mm, sq_tok[it - 2])
                s.activation(sl, xsl, AF.Square)
                sq_sc = s.tok
                t.matmul(ps_a[:], ONES, xsl, start=(k == 0), stop=(k == NK - 1))
                t.wait_ge(self.sc.h, sq_sc)
                last = t.matmul(ps_b[:], ONES, sl, start=(k == 0),
                                stop=(k == NK - 1))
                sq_tok.append(self._mm(last))
            mm_st = self.mm.n
            self.ln_mm_stats = mm_st
            v.wait(self.mm, mm_st)
            v.tensor_single_scalar(MT, ps_a[:], 1.0 / D, OP.mult)
            v.tensor_tensor(VT, MT, MT, OP.mult)
            v.scalar_tensor_tensor(VT, ps_b[:], 1.0 / D, VT, OP.mult,
                                   OP.subtract)
            if need_rn:
                v.tensor_copy(self.MROW[:, cs], MT[0:1, :])
                v.tensor_copy(NRS[:, cs], VT[0:8, :])
            ev_v = v.tok
            prev_ev[p] = ev_v
            self.ln_ev_stats = ev_v
            s.wait(self.ev, ev_v)
            s.activation(VT, VT, AF.Ln, bias=self.EPSC[:, :])
            s.activation(VT, VT, AF.Exp, scale=-0.5)
            sc_r = s.tok
            v.wait(self.sc, sc_r)
            if need_rn:
                v.tensor_copy(self.RROW[:, cs], VT[0:1, :])
                v.tensor_tensor(NRS[:, cs], NRS[:, cs], VT[0:8, :], OP.mult)
                v.tensor_tensor(NRS[:, cs], NRS[:, cs], VT[0:8, :], OP.mult)
                v.tensor_single_scalar(NRS[:, cs], NRS[:, cs], float(D),
                                       OP.mult)
            TT = self.RESG[:, 0:TC]
            for k in range(NK):
                xsl = XBF[:, k * T + tc * TC: k * T + (tc + 1) * TC]
                v.tensor_tensor(TT, xsl, MT, OP.subtract)
                v.tensor_tensor(xsl, TT, VT, OP.mult)
            ev_norm_prev = v.tok
            self.ln_ev_tc[tc] = ev_norm_prev
        self.ln_done = ev_norm_prev

    def emit_l0ffn(self, res_src, res_dst, wgu, wdn, h_scr):
        t, v, s, g, q = self.t, self.v, self.s, self.g, self.q
        self.emit_ln(res_src)
        XBF, WBUF, TMPF, HEV = self.XBF, self.WBUF, self.TMPF, self.HEV
        big = self.BIGB
        mm_tok = [0] * 128
        ev_tok = [0] * 64
        sc_tok = [0] * 64
        st_tok = [0] * 64
        for jp in range(64):
            for half, j in ((0, jp), (1, jp + 64)):
                idx = 2 * jp + half
                slot = idx % 4
                wsl = WBUF[:, slot * 2048:(slot + 1) * 2048]
                if idx >= 4:
                    g.wait_ge(self.mm.h, mm_tok[idx - 4])
                tok = self._dma(g, self.wsl[slot], wsl,
                                wgu[j].rearrange("p k c -> p (k c)"))
                banks = self.ps[0:4] if half == 0 else self.ps[4:8]
                t.wait_ge(self.wsl[slot].h, tok)
                if jp > 0:
                    if half == 0:
                        t.wait_ge(self.sc.h, sc_tok[jp - 1])
                    else:
                        t.wait_ge(self.ev.h, ev_tok[jp - 1])
                if idx == 1:
                    t.wait_ge(self.ev.h, self.ln_ev_stats)
                for tc in range(NTC):
                    if idx < 2:
                        t.wait_ge(self.ev.h, self.ln_ev_tc[tc])
                    for k in range(NK):
                        last = t.matmul(banks[tc][:], wsl[:, k * 128:(k + 1) * 128],
                                        XBF[:, k * T + tc * TC: k * T + (tc + 1) * TC],
                                        start=(k == 0), stop=(k == NK - 1))
                mm_tok[idx] = self._mm(last)
            s.wait(self.mm, mm_tok[2 * jp])
            if jp > 0:
                s.wait(self.ev, ev_tok[jp - 1])
            for tc in range(NTC):
                s.activation(TMPF[:, tc * TC:(tc + 1) * TC], self.ps[tc][:], AF.Silu)
            sc_tok[jp] = s.tok
            hslot = HEV[:, (jp % 2) * 2048:(jp % 2 + 1) * 2048]
            v.wait(self.mm, mm_tok[2 * jp + 1])
            v.wait(self.sc, sc_tok[jp])
            if jp >= 2:
                v.wait(self.hsl[jp % 2], st_tok[jp - 2])
            for tc in range(NTC):
                v.tensor_tensor(hslot[:, tc * TC:(tc + 1) * TC],
                                TMPF[:, tc * TC:(tc + 1) * TC],
                                self.ps[4 + tc][:], OP.mult)
            ev_tok[jp] = v.tok
            q.wait_ge(self.ev.h, ev_tok[jp])
            st_tok[jp] = self._dma(q, self.hsl[jp % 2], h_scr[jp], hslot)
        g.wait_ge(self.mm.h, mm_tok[125])
        wd_pre0 = self._dma(g, self.wsl[0], WBUF[:, 0:4096],
                            wdn[0, :, 32:64, :].rearrange("p i c -> p (i c)"))
        g.wait_ge(self.mm.h, mm_tok[127])
        wd_pre1 = self._dma(g, self.wsl[1], WBUF[:, 4096:8192],
                            wdn[0, :, 0:32, :].rearrange("p i c -> p (i c)"))
        for ib in range(32, NIB):
            g.wait_ge(self.hsl[ib % 2].h, st_tok[ib])
            self._dma(g, self.hld, big[:, (ib - 32) * 1024:(ib - 31) * 1024],
                      h_scr[ib][:, 0:1024])
        h_stored = (self.hsl[0].n, self.hsl[1].n)
        dn_ev = [0, 0]
        dn_st = [0, 0]
        wd_mm = [0, 0]
        for tc2 in range(2):
            cs2 = slice(tc2 * 1024, (tc2 + 1) * 1024)
            q.wait_ge(self.hsl[0].h, h_stored[0])
            q.wait_ge(self.hsl[1].h, h_stored[1])
            q.wait_ge(self.mm.h, mm_tok[127])
            self.nc.scalar.wait_ge(self.hsl[0].h, h_stored[0])
            self.nc.scalar.wait_ge(self.hsl[1].h, h_stored[1])
            self.nc.scalar.wait_ge(self.mm.h, mm_tok[127])
            if tc2 > 0:
                q.wait_ge(self.ev.h, self.ev.n)
                self.nc.scalar.wait_ge(self.ev.h, self.ev.n)
            for ib in range(NIB):
                dst = (XBF[:, ib * 1024:(ib + 1) * 1024] if ib < 32
                       else big[:, (ib - 32) * 1024:(ib - 31) * 1024])
                if tc2 == 0 and ib >= 32:
                    continue
                eng = q if ib % 2 == 0 else self.nc.scalar
                self._dma(eng, self.xq, dst, h_scr[ib][:, cs2])
            h_ready = self.xq.n
            t.wait_ge(self.hld.h, self.hld.n)
            if tc2 > 0:
                t.wait_ge(self.xq.h, h_ready)
            for dblk in range(NK):
                b0 = self.ps[2 * (dblk % 2)]
                b1 = self.ps[2 * (dblk % 2) + 1]
                for pos_h, hh in enumerate((1, 0)):
                    pos = 2 * dblk + pos_h
                    slot = pos % 2
                    wsl = WBUF[:, slot * 4096:(slot + 1) * 4096]
                    if tc2 == 0 and pos == 0:
                        tok = wd_pre0
                    elif tc2 == 0 and pos == 1:
                        tok = wd_pre1
                    else:
                        g.wait_ge(self.mm.h, wd_mm[slot])
                        tok = self._dma(g, self.wsl[slot], wsl,
                                        wdn[dblk, :, hh * 32:(hh + 1) * 32,
                                            :].rearrange("p i c -> p (i c)"))
                    t.wait_ge(self.wsl[slot].h, tok)
                    if pos_h == 0 and (dblk >= 2 or tc2 > 0):
                        t.wait_ge(self.ev.h, dn_ev[dblk % 2])
                    if tc2 == 0 and pos == 1:
                        t.wait_ge(self.xq.h, h_ready)
                    for ib in range(32):
                        gib = hh * 32 + ib
                        hsrc = (XBF[:, gib * 1024:(gib + 1) * 1024] if gib < 32
                                else big[:, (gib - 32) * 1024:(gib - 31) * 1024])
                        st_ = (pos_h == 0 and ib == 0)
                        sp_ = (pos_h == 1 and ib == 31)
                        t.matmul(b0[:], wsl[:, ib * 128:(ib + 1) * 128],
                                 hsrc[:, 0:512], start=st_, stop=sp_)
                        last = t.matmul(b1[:], wsl[:, ib * 128:(ib + 1) * 128],
                                        hsrc[:, 512:1024], start=st_, stop=sp_)
                    wd_mm[slot] = self._mm(last)
                mm_done = wd_mm[1]
                rslot = self.RESG[:, (dblk % 2) * 1024:(dblk % 2 + 1) * 1024]
                if dblk >= 2 or tc2 > 0:
                    self.nc.scalar.wait_ge(self.ssl[dblk % 2].h, dn_st[dblk % 2])
                rtok = self._dma(self.nc.scalar, self.rsl[dblk % 2], rslot,
                                 res_src[dblk * 128:(dblk + 1) * 128, cs2])
                v.wait(self.mm, mm_done)
                v.wait(self.rsl[dblk % 2], rtok)
                v.scalar_tensor_tensor(rslot[:, 0:512], b0[:], 0.5, rslot[:, 0:512],
                                       OP.mult, OP.add)
                v.scalar_tensor_tensor(rslot[:, 512:1024], b1[:], 0.5,
                                       rslot[:, 512:1024], OP.mult, OP.add)
                dn_ev[dblk % 2] = v.tok
                q.wait_ge(self.ev.h, dn_ev[dblk % 2])
                dn_st[dblk % 2] = self._dma(
                    q, self.ssl[dblk % 2],
                    res_dst[dblk * 128:(dblk + 1) * 128, cs2], rslot)

    def emit_l0attn(self, res_src, res_dst, bt, ct, dcol):
        t, v, s, g, q = self.t, self.v, self.s, self.g, self.q
        self.emit_ln(res_src)
        XBF, WBUF = self.XBF, self.WBUF
        STATE = self.BIGB
        dc = self._dma(q, self.xq, self.CON[:, 0:16], dcol)
        DIAG = self.HEV
        g.wait_ge(self.xq.h, dc)
        g.wait_ge(self.mm.h, self.mm.n)
        for dblk in range(NK):
            g.tensor_scalar(DIAG[:, dblk * 128:(dblk + 1) * 128], self.ID128[:],
                            self.CON[:, dblk:dblk + 1], None, OP.mult)
        gp_diag = self.gp.n
        t.wait_ge(self.gp.h, gp_diag)
        mm_tok = [0] * 8
        sc_tok = [0] * 8
        for nb in range(8):
            slot = nb % 3
            wsl = WBUF[:, slot * 2048: slot * 2048 + 2048]
            if nb >= 3:
                g.wait_ge(self.mm.h, mm_tok[nb - 3])
            tok = self._dma(g, self.wsl[slot], wsl,
                            bt[nb].rearrange("p k c -> p (k c)"))
            banks = self.ps[0:4] if nb % 2 == 0 else self.ps[4:8]
            t.wait_ge(self.wsl[slot].h, tok)
            if nb >= 2:
                t.wait_ge(self.sc.h, sc_tok[nb - 2])
            if nb == 1:
                t.wait_ge(self.ev.h, self.ln_ev_stats)
            for tc in range(NTC):
                if nb < 2:
                    t.wait_ge(self.ev.h, self.ln_ev_tc[tc])
                for k in range(NK):
                    last = t.matmul(banks[tc][:], wsl[:, k * 128:(k + 1) * 128],
                                    XBF[:, k * T + tc * TC: k * T + (tc + 1) * TC],
                                    start=(k == 0), stop=(k == NK - 1))
            mm_tok[nb] = self._mm(last)
            s.wait(self.mm, mm_tok[nb])
            for tc in range(NTC):
                s.activation(STATE[:, nb * T + tc * TC: nb * T + (tc + 1) * TC],
                             banks[tc][:], AF.Tanh)
            sc_tok[nb] = s.tok
        dn_ev = [0, 0]
        dn_st4 = [0, 0, 0, 0]
        ct_mm = [0, 0]
        v.wait(self.xq, dc)
        for dblk in range(NK):
            slot = dblk % 2
            wsl = WBUF[:, 6144 + slot * 1024: 6144 + slot * 1024 + 1024]
            if dblk >= 2:
                g.wait_ge(self.mm.h, ct_mm[slot])
            tok = self._dma(g, self.wsl[4 + slot], wsl,
                            ct[dblk].rearrange("p n c -> p (n c)"))
            banks = self.ps[0:4] if dblk % 2 == 0 else self.ps[4:8]
            t.wait_ge(self.wsl[4 + slot].h, tok)
            if dblk >= 2:
                t.wait_ge(self.ev.h, dn_ev[slot])
            if dblk < 2:
                t.wait_ge(self.sc.h, sc_tok[7])
            for nb in range(8):
                for tc in range(NTC):
                    t.matmul(banks[tc][:], wsl[:, nb * 128:(nb + 1) * 128],
                             STATE[:, nb * T + tc * TC: nb * T + (tc + 1) * TC],
                             start=(nb == 0), stop=False)
            for tc in range(NTC):
                last = t.matmul(banks[tc][:], DIAG[:, dblk * 128:(dblk + 1) * 128],
                                XBF[:, dblk * T + tc * TC: dblk * T + (tc + 1) * TC],
                                start=False, stop=True)
            ct_mm[slot] = self._mm(last)
            for tc in range(NTC):
                rslot = self.RESG[:, tc * TC:(tc + 1) * TC]
                if dblk >= 1:
                    self.nc.scalar.wait_ge(self.ssl[tc].h, dn_st4[tc])
                rtok = self._dma(self.nc.scalar, self.rsl[tc], rslot,
                                 res_src[dblk * 128:(dblk + 1) * 128,
                                         tc * TC:(tc + 1) * TC])
                v.wait(self.mm, ct_mm[slot])
                v.wait(self.rsl[tc], rtok)
                v.tensor_tensor(rslot, banks[tc][:], rslot, OP.add)
                dn_ev[slot] = v.tok
                q.wait_ge(self.ev.h, dn_ev[slot])
                dn_st4[tc] = self._dma(
                    q, self.ssl[tc],
                    res_dst[dblk * 128:(dblk + 1) * 128, tc * TC:(tc + 1) * TC],
                    rslot)

    def emit_l1ffn(self, res_src, res_dst, selw, mu, upd, dgn, bias, debias,
                   censlo, first=True):
        t, v, s, g, q = self.t, self.v, self.s, self.g, self.q
        self.emit_ln(res_src, need_rn=True)
        XBF, WBUF, CON, BF = self.XBF, self.WBUF, self.CON, self.BIG_F
        SELW = WBUF[:, 0:1024]
        UPD = WBUF[:, 1024:2048]
        MUB = WBUF[:, 2048:2176]
        DGN = WBUF[0:64, 2560:4608]
        BIA = WBUF[0:8, 4608:6656]
        V2F = [BF[0:64, 0:512], BF[0:64, 512:1024]]
        MODX = [BF[0:64, 1024:1536], BF[0:64, 1536:2048]]
        QSF = [BF[0:8, 3072:3584], BF[0:8, 3584:4096]]
        DEN = [BF[0:8, 4096:4608], BF[0:8, 4608:5120]]
        GF = [BF[0:8, 5120:5632], BF[0:8, 5632:6144]]
        FMS = [BF[0:8, 6144:6656], BF[0:8, 6656:7168]]
        FGF = [BF[0:8, 7168:7680], BF[0:8, 7680:8192]]
        MMX = [self.HEV[0:64, 1024:1536], self.HEV[0:64, 1536:2048]]
        FGB = [WBUF[0:8, 6656:7168], WBUF[0:8, 7168:7680]]
        SMB = self.SM.ap().bitcast(BF16)
        MCB = [SMB[0:1, 0:512], SMB[0:1, 512:1024]]
        ONE1B = SMB[0:1, 1024:1152]
        TMJ = [BF[:, 8192:8224], BF[:, 8224:8256]]
        IMW = [BF[:, 8256:8288], BF[:, 8288:8320]]
        PSW = [BF[:, 8320:8352], BF[:, 8352:8384]]
        FMW = [BF[:, 8384:8416], BF[:, 8416:8448]]
        MX = [BF[:, 8448:8452], BF[:, 8452:8456]]
        RC = [BF[0:1, 8456:8968], BF[0:1, 8968:9480]]
        MC = [BF[0:1, 9480:9992], BF[0:1, 9992:10504]]
        MUF = WBUF[:, 2176:2304]
        DBT = WBUF[:, 2304:2432]
        MSQ = BF[:, 10760:10888]
        NEGCT = BF[0:64, 10888:10896]
        T81 = BF[0:8, 10896:10897]
        NEGC = CON[0:64, 45:46]
        NEGC2 = CON[0:64, 46:47]
        M2C = CON[0:8, 47:48]
        GBIAS = CON[0:8, 49:50]
        CENT = CON[0:8, 50:51]
        SLOP = CON[0:8, 51:52]
        g.memset(ONE1B, 1.0)
        gp_one1b = self.gp.n
        for dst, src in ((SELW, selw.rearrange("p a b -> p (a b)")),
                         (UPD, upd.rearrange("p a b -> p (a b)")),
                         (MUB, mu.rearrange("p a b -> p (a b)")),
                         (DGN, dgn), (BIA, bias)):
            self._dma(g, self.wgrp, dst, src)
        t.wait_ge(self.gp.h, gp_one1b)
        if first:
            for dst, src in ((MUF, mu.rearrange("p a b -> p (a b)")),
                             (DBT, debias.rearrange("p a b -> p (a b)"))):
                self._dma(g, self.wgrp, dst, src)
            self._dma(q, self.xq, CON[0:8, 50:52], censlo)
        t.wait_ge(self.wgrp.h, self.wgrp.n)
        if first:
            xld = self.xq.n
            s.wait(self.xq, xld)
            s.wait(self.wgrp, self.wgrp.n)
            s.activation(MSQ, MUF, AF.Square)
            sc_msq = s.tok
            t.wait_ge(self.sc.h, sc_msq)
            t.wait_ge(self.ev.h, self.ln_ev_stats)
            for k in range(NK):
                t.matmul(self.ps[4][0:8, 0:1], MSQ[:, k * 8:(k + 1) * 8],
                         self.ONEF[:], start=(k == 0), stop=(k == NK - 1))
                t.matmul(self.ps[5][0:64, 0:8], SELW[:, k * 64:(k + 1) * 64],
                         MUF[:, k * 8:(k + 1) * 8],
                         start=(k == 0), stop=(k == NK - 1))
                last = t.matmul(self.ps[6][0:64, 0:8],
                                UPD[:, k * 64:(k + 1) * 64],
                                DBT[:, k * 8:(k + 1) * 8],
                                start=(k == 0), stop=(k == NK - 1))
            mm_pro = self._mm(last)
            s.wait(self.mm, mm_pro)
            s.activation(M2C, self.ps[4][0:8, 0:1], AF.Copy)
            sc_m2 = s.tok
            v.wait(self.mm, mm_pro)
            v.tensor_tensor(NEGCT, self.ps[5][0:64, 0:8], self.ESUM, OP.mult)
            v.tensor_reduce(NEGC, NEGCT, mybir.AxisListType.X, OP.add)
            v.tensor_single_scalar(NEGC, NEGC, -1.0, OP.mult)
            v.tensor_tensor(NEGCT, self.ps[6][0:64, 0:8], self.ESUM, OP.mult)
            v.tensor_reduce(NEGC2, NEGCT, mybir.AxisListType.X, OP.add)
            v.tensor_single_scalar(NEGC2, NEGC2, -1.0, OP.mult)
            v.tensor_tensor(T81, SLOP, CENT, OP.mult)
            v.tensor_single_scalar(GBIAS, T81, -1.0, OP.mult)
            ev_pro = v.tok
            t.wait_ge(self.ev.h, ev_pro)
            t.wait_ge(self.sc.h, sc_m2)
            s.wait(self.ev, ev_pro)

        st = [dict() for _ in range(NTC)]
        dg = {"ev0": 0, "ev1": 0, "st0": 0, "st1": 0}

        def P1(tc):
            p = tc % 2
            cs = slice(tc * TC, (tc + 1) * TC)
            pA, pN, pS = self.ps[0 + p], self.ps[4 + p], self.ps[6 + p]
            o = st[tc]
            if tc >= 2:
                po = st[tc - 2]
                t.wait_ge(self.ev.h, po["ev_mx"])
                s.wait(self.ev, po["ev_fg"])
            for k in range(NK):
                last = t.matmul(pA[0:64, :], SELW[:, k * 64:(k + 1) * 64],
                                XBF[:, k * T + tc * TC: k * T + (tc + 1) * TC],
                                start=(k == 0), stop=(k == NK - 1))
            o["mm_a"] = self._mm(last)
            s.wait(self.mm, o["mm_a"])
            s.activation(V2F[p], pA[0:64, :], AF.Square, bias=NEGC)
            sc_v2 = s.tok
            t.wait_ge(self.sc.h, sc_v2)
            if tc < 2:
                t.wait_ge(self.ev.h, self.ln_ev_stats)
            if tc >= 2:
                t.wait_ge(self.ev.h, st[tc - 2]["ev_rows"])
            o["mm_num"] = self._mm(t.matmul(pN[0:8, :], self.ESUM, V2F[p],
                                            start=True, stop=True))
            for k in range(NK):
                last = t.matmul(pA[0:8, :], MUB[:, k * 8:(k + 1) * 8],
                                XBF[:, k * T + tc * TC: k * T + (tc + 1) * TC],
                                start=(k == 0), stop=(k == NK - 1))
            o["mm_q"] = self._mm(last)
            s.wait(self.mm, o["mm_q"])
            s.activation(QSF[p], pA[0:8, :], AF.Copy)
            o["sc_q"] = s.tok
            t.wait_ge(self.sc.h, o["sc_q"])
            for k in range(NK):
                last = t.matmul(pA[0:64, :], UPD[:, k * 64:(k + 1) * 64],
                                XBF[:, k * T + tc * TC: k * T + (tc + 1) * TC],
                                start=(k == 0), stop=(k == NK - 1))
            o["mm_u"] = self._mm(last)
            v.wait(self.mm, o["mm_u"])
            v.tensor_scalar(MODX[p], pA[0:64, :], NEGC2, None, OP.add)
            o["ev_mx"] = v.tok
            v.wait(self.sc, o["sc_q"])
            v.scalar_tensor_tensor(DEN[p], QSF[p], -2.0,
                                   self.BIG_F[0:8, 11008 + tc * TC:
                                              11008 + (tc + 1) * TC],
                                   OP.mult, OP.add)
            v.tensor_scalar(DEN[p], DEN[p], M2C, None, OP.add)
            o["ev_den"] = v.tok
            s.wait(self.ev, o["ev_den"])
            s.activation(DEN[p], DEN[p], AF.Ln)
            s.activation(DEN[p], DEN[p], AF.Exp, scale=-1.0)
            sc_den = s.tok
            v.wait(self.sc, sc_den)
            v.wait(self.mm, o["mm_num"])
            v.tensor_tensor(QSF[p], pN[0:8, :], DEN[p], OP.mult)
            o["ev_sco"] = v.tok
            s.wait(self.ev, o["ev_sco"])
            s.activation(GF[p], QSF[p], AF.Sigmoid, scale=SLOP, bias=GBIAS)
            o["sc_g"] = s.tok

        def P2(tc):
            p = tc % 2
            cs = slice(tc * TC, (tc + 1) * TC)
            pN, pS = self.ps[4 + p], self.ps[6 + p]
            o = st[tc]
            t.wait_ge(self.sc.h, o["sc_g"])
            if tc >= 2:
                t.wait_ge(self.ev.h, st[tc - 2]["ep_ev"])
            for i in range(4):
                last = t.transpose(pS[:, i * 8:(i + 1) * 8],
                                   GF[p][:, i * 128:(i + 1) * 128],
                                   self.ID128[0:8, 0:8])
            mm_tr = self._mm(last)
            v.wait(self.mm, mm_tr)
            if tc >= 2:
                v.wait(self.mm, st[tc - 2]["mm_tb"])
            for i in range(4):
                v.tensor_reduce(MX[p][:, i:i + 1], pS[:, i * 8:(i + 1) * 8],
                                mybir.AxisListType.X, OP.max)
            for i in range(4):
                v.tensor_scalar(IMW[p][:, i * 8:(i + 1) * 8],
                                pS[:, i * 8:(i + 1) * 8],
                                MX[p][:, i:i + 1], None, OP.is_equal)
            for i in range(4):
                v.tensor_tensor_scan(PSW[p][:, i * 8:(i + 1) * 8],
                                     self.ONE8[:], IMW[p][:, i * 8:(i + 1) * 8],
                                     0.0, OP.mult, OP.add)
            v.tensor_single_scalar(PSW[p], PSW[p], 1.0, OP.is_equal)
            v.tensor_tensor(FMW[p], IMW[p], PSW[p], OP.mult)
            o["ev_fm"] = v.tok
            t.wait_ge(self.ev.h, o["ev_fm"])
            for i in range(4):
                last = t.transpose(pS[0:8, i * 128:(i + 1) * 128],
                                   FMW[p][:, i * 8:(i + 1) * 8], self.ID128[:])
            o["mm_tb"] = self._mm(last)
            s.wait(self.mm, o["mm_tb"])
            s.activation(FMS[p], pS[0:8, :], AF.Copy)
            sc_fm = s.tok
            v.wait(self.sc, sc_fm)
            if tc >= 2:
                v.wait(self.mm, st[tc - 2]["ep_mm"])
            v.scalar_tensor_tensor(FGF[p], GF[p], 0.5, FMS[p], OP.mult, OP.mult)
            o["ev_fg"] = v.tok
            s.wait(self.ev, o["ev_fg"])
            if tc >= 2:
                s.wait(self.mm, st[tc - 2]["ep_mm"])
            s.activation(FGB[p], FGF[p], AF.Copy)
            sc_fgb = s.tok
            t.wait_ge(self.ev.h, o["ev_sco"])
            t.wait_ge(self.ev.h, o["ev_fg"])
            mm_s = self._mm(t.matmul(pN[0:1, :], self.ONEF[0:8, :], FGF[p],
                                     start=True, stop=True))
            t.wait_ge(self.sc.h, sc_fm)
            mm_e2 = self._mm(t.matmul(pS[0:64, :], self.E2F, FGF[p],
                                      start=True, stop=True))
            v.wait(self.mm, mm_e2)
            v.tensor_tensor(MMX[p], MODX[p], pS[0:64, :], OP.mult)
            o["ev_mmx"] = v.tok
            v.wait(self.mm, mm_s)
            v.tensor_scalar(RC[p], pN[0:1, :], -1.0, 0.5, OP.mult, OP.add)
            v.tensor_tensor(RC[p], self.RROW[:, cs], RC[p], OP.mult)
            v.scalar_tensor_tensor(MCB[p], self.MROW[:, cs], -1.0, RC[p],
                                   OP.mult, OP.mult)
            v.tensor_single_scalar(RC[p], RC[p], 1.0, OP.add)
            o["ev_rows"] = v.tok
            t.wait_ge(self.ev.h, o["ev_mmx"])
            t.wait_ge(self.ev.h, o["ev_rows"])
            mm_bc = self._mm(t.matmul(pS[:], self.ONE1[:], RC[p],
                                      start=True, stop=True))
            t.wait_ge(self.sc.h, sc_fgb)
            for dblk in range(NK):
                bank = self.ps[2 + (dblk % 2)]
                if dg["ev" + str(dblk % 2)]:
                    t.wait_ge(self.ev.h, dg["ev" + str(dblk % 2)])
                t.matmul(bank[:], DGN[:, dblk * 128:(dblk + 1) * 128], MMX[p],
                         start=True, stop=False)
                t.matmul(bank[:], BIA[:, dblk * 128:(dblk + 1) * 128], FGB[p],
                         start=False, stop=False)
                mm_d = self._mm(t.matmul(bank[:], ONE1B, MCB[p],
                                         start=False, stop=True))
                rslot = self.RESG[:, (dblk % 2) * TC:(dblk % 2 + 1) * TC]
                if dg["st" + str(dblk % 2)]:
                    self.nc.scalar.wait_ge(self.ssl[dblk % 2].h,
                                           dg["st" + str(dblk % 2)])
                rtok = self._dma(self.nc.scalar, self.rsl[dblk % 2], rslot,
                                 res_src[dblk * 128:(dblk + 1) * 128, cs])
                v.wait(self.mm, mm_d)
                v.wait(self.rsl[dblk % 2], rtok)
                v.tensor_tensor(rslot, rslot, pS[:], OP.mult)
                v.tensor_tensor(rslot, rslot, bank[:], OP.add)
                dg["ev" + str(dblk % 2)] = v.tok
                q.wait_ge(self.ev.h, dg["ev" + str(dblk % 2)])
                dg["st" + str(dblk % 2)] = self._dma(
                    q, self.ssl[dblk % 2],
                    res_dst[dblk * 128:(dblk + 1) * 128, cs], rslot)
            o["ep_ev"] = v.tok
            o["ep_mm"] = self.mm.n

        P1(0)
        P1(1)
        P2(0)
        P1(2)
        P2(1)
        P1(3)
        P2(2)
        P2(3)

    def emit_l1attn(self, res_src, res_dst, wx, wz, wp, wo, wdt, convw, convb,
                    alog, dval, bdt):
        t, v, s, g, q = self.t, self.v, self.s, self.g, self.q
        self.emit_ln(res_src)
        XBF, WBUF, CON, BF = self.XBF, self.WBUF, self.CON, self.BIG_F
        XF = self.XBF_F
        ZS = self.BIGB[:, 16384:24576]
        XAB = self.BIGB[:, 24576:32768]
        WDTB = WBUF[:, 4608:4640]
        ACOL = CON[:, 16:20]
        DCOL2 = CON[:, 20:24]
        CVB = CON[:, 24:28]
        CVW = CON[:, 28:44]
        BDTC = CON[0:32, 44:45]
        ATMP = CON[:, 60:64]
        DELT = self.SM[0:32, 0:512]
        SP1 = self.SM[0:32, 512:1024]
        HCAR = self.TMPF[:, 0:4]
        for dst, src in ((CVW, convw), (CVB, convb), (ATMP, alog),
                         (DCOL2, dval), (BDTC, bdt)):
            self._dma(q, self.xq, dst, src)
        xq_tok = self.xq.n
        s.wait(self.xq, xq_tok)
        s.activation(ACOL, ATMP, AF.Exp)
        s.mul(ACOL, ACOL, -1.0)
        sc_acol = s.tok
        sc_tok = [0] * 8
        mm_tok = [0] * 8
        for ob in range(8):
            w_src = wx[ob] if ob < 4 else wz[ob - 4]
            slot = ob % 2
            wsl = WBUF[:, slot * 2048:(slot + 1) * 2048]
            if ob >= 2:
                g.wait_ge(self.mm.h, mm_tok[ob - 2])
            tok = self._dma(g, self.wsl[slot], wsl,
                            w_src.rearrange("p k c -> p (k c)"))
            banks = self.ps[0:4] if ob % 2 == 0 else self.ps[4:8]
            t.wait_ge(self.wsl[slot].h, tok)
            if ob >= 2:
                t.wait_ge(self.sc.h, sc_tok[ob - 2])
            if ob == 1:
                t.wait_ge(self.ev.h, self.ln_ev_stats)
            for tc in range(NTC):
                if ob < 2:
                    t.wait_ge(self.ev.h, self.ln_ev_tc[tc])
                for k in range(NK):
                    last = t.matmul(banks[tc][:], wsl[:, k * 128:(k + 1) * 128],
                                    XBF[:, k * T + tc * TC: k * T + (tc + 1) * TC],
                                    start=(k == 0), stop=(k == NK - 1))
            mm_tok[ob] = self._mm(last)
            s.wait(self.mm, mm_tok[ob])
            for tc in range(NTC):
                if ob < 4:
                    s.activation(BF[:, ob * 2048 + tc * TC: ob * 2048 + (tc + 1) * TC],
                                 banks[tc][:], AF.Copy)
                else:
                    s.activation(ZS[:, (ob - 4) * 2048 + tc * TC:
                                    (ob - 4) * 2048 + (tc + 1) * TC],
                                 banks[tc][:], AF.Silu)
            sc_tok[ob] = s.tok
        xz_done_mm = self.mm.n
        acc = self.TMPF[:, 0:2048]
        conv_sc = [0] * 4
        for ct4 in range(4):
            XP = BF[:, ct4 * 2048:(ct4 + 1) * 2048]
            v.wait(self.sc, sc_tok[ct4])
            if ct4 > 0:
                v.wait(self.sc, conv_sc[ct4 - 1])
            v.tensor_scalar(acc, XP, CVW[:, ct4 * 4 + 3: ct4 * 4 + 4], None, OP.mult)
            v.scalar_tensor_tensor(acc[:, 1:2048], XP[:, 0:2047],
                                   CVW[:, ct4 * 4 + 2: ct4 * 4 + 3],
                                   acc[:, 1:2048], OP.mult, OP.add)
            v.scalar_tensor_tensor(acc[:, 2:2048], XP[:, 0:2046],
                                   CVW[:, ct4 * 4 + 1: ct4 * 4 + 2],
                                   acc[:, 2:2048], OP.mult, OP.add)
            v.scalar_tensor_tensor(acc[:, 3:2048], XP[:, 0:2045],
                                   CVW[:, ct4 * 4 + 0: ct4 * 4 + 1],
                                   acc[:, 3:2048], OP.mult, OP.add)
            ev_conv = v.tok
            s.wait(self.ev, ev_conv)
            s.activation(XAB[:, ct4 * 2048:(ct4 + 1) * 2048], acc, AF.Silu,
                         bias=CVB[:, ct4:ct4 + 1])
            s.activation(XP, acc, AF.Silu, bias=CVB[:, ct4:ct4 + 1])
            conv_sc[ct4] = s.tok
        SMB = self.SM.ap().bitcast(BF16)
        wo_dst = ([self.HEV[:, i * 512:(i + 1) * 512] for i in range(8)] +
                  [WBUF[:, 4640 + i * 512: 4640 + (i + 1) * 512]
                   for i in range(6)] +
                  [SMB[:, i * 512:(i + 1) * 512] for i in range(2)])
        g.wait_ge(self.mm.h, xz_done_mm)
        for ob in range(9):
            self._dma(g, self.wgrp, WBUF[:, ob * 512:(ob + 1) * 512],
                      wp[ob].rearrange("p k c -> p (k c)"))
        self._dma(g, self.wgrp, WDTB, wdt)
        g.wait_ge(self.mm.h, self.ln_mm_stats)
        for dblk in range(NK):
            self._dma(g, self.wgrp, wo_dst[dblk],
                      wo[dblk].rearrange("p k c -> p (k c)"))
        wp_tok = self.wgrp.n
        t.wait_ge(self.wgrp.h, wp_tok)
        t.wait_ge(self.sc.h, conv_sc[3])
        t.wait_ge(self.sc.h, sc_acol)
        YBF = self.XBF[:, 24576:26624]
        DTF = [self.XBF[:, 26624:27136], self.XBF[:, 27136:27648],
               self.XBF[:, 27648:28160]]
        DELT = [self.TMPF[0:32, 4:516], self.TMPF[0:32, 516:1028],
                self.TMPF[0:32, 1028:1540]]
        SP1 = self.RESG[0:32, 0:512]
        PAR_B = [XF[:, 0:2048], XF[:, 2048:4096], XF[:, 4096:6144]]
        PAR_C = [XF[:, 6144:8192], XF[:, 8192:10240], XF[:, 10240:12288]]
        ABAR = XF[:, 12288:14336]
        HST = XF[:, 14336:16384]
        st = [dict() for _ in range(NTC)]
        par_bank_sc = [0, 0, 0]
        dbc_ev = [0, 0]
        wo_mm = [0, 0]
        wo_st = [0, 0]
        wo_ev = [0, 0]

        def P1(u):
            pu = u % 3
            o = st[u]
            if u >= 3:
                s.wait(self.ev, st[u - 3]["ev_y"])
            for ob in range(9):
                bank = self.ps[ob % 3]
                if par_bank_sc[ob % 3]:
                    t.wait_ge(self.sc.h, par_bank_sc[ob % 3])
                for kc in range(4):
                    last = t.matmul(bank[0:128, :],
                                    WBUF[:, ob * 512 + kc * 128:
                                         ob * 512 + (kc + 1) * 128],
                                    XAB[:, kc * 2048 + u * TC:
                                        kc * 2048 + (u + 1) * TC],
                                    start=(kc == 0), stop=(kc == 3))
                mm_par = self._mm(last)
                s.wait(self.mm, mm_par)
                if ob == 0:
                    s.activation(DTF[pu], bank[0:128, :], AF.Copy)
                elif ob <= 4:
                    s.activation(PAR_B[pu][:, (ob - 1) * TC:ob * TC],
                                 bank[0:128, :], AF.Copy)
                else:
                    s.activation(PAR_C[pu][:, (ob - 5) * TC:(ob - 4) * TC],
                                 bank[0:128, :], AF.Copy)
                par_bank_sc[ob % 3] = s.tok
                if ob == 0:
                    o["sc_dtf"] = s.tok
            o["sc_par"] = s.tok
            t.wait_ge(self.sc.h, o["sc_dtf"])
            if u >= 1:
                t.wait_ge(self.ev.h, st[u - 1]["ev_d0"])
            o["mm_dt"] = self._mm(t.matmul(self.ps[7][0:32, :], WDTB, DTF[pu],
                                           start=True, stop=True))
            v.wait(self.mm, o["mm_dt"])
            if u >= 3:
                v.wait(self.mm, st[u - 3]["mm_dbc"])
            v.tensor_scalar(DELT[pu], self.ps[7][0:32, :], BDTC, None, OP.add)
            o["ev_d0"] = v.tok
            s.wait(self.ev, o["ev_d0"])
            if u >= 1:
                s.wait(self.ev, st[u - 1]["ev_delt"])
            s.activation(SP1, DELT[pu], AF.Abs)
            s.activation(SP1, SP1, AF.Exp, scale=-1.0)
            s.activation(SP1, SP1, AF.Ln, bias=self.ONEF[0:32, :])
            sc_sp = s.tok
            v.wait(self.sc, sc_sp)
            v.tensor_scalar(DELT[pu], DELT[pu], 0.0, None, OP.max)
            v.tensor_tensor(DELT[pu], DELT[pu], SP1, OP.add)
            o["ev_delt"] = v.tok

        def P2(u):
            pu = u % 3
            ccs = slice(u * TC, (u + 1) * TC)
            o = st[u]
            t.wait_ge(self.ev.h, o["ev_delt"])
            if u >= 1:
                v.wait(self.mm, st[u - 1]["wo_mm"])
                s.wait(self.ev, st[u - 1]["ev_y"])
            v.wait(self.sc, o["sc_par"])
            mm_dbc = 0
            for ct4 in range(4):
                dbc = self.ps[3 + (ct4 % 2)]
                if dbc_ev[ct4 % 2]:
                    t.wait_ge(self.ev.h, dbc_ev[ct4 % 2])
                mm_dbc = self._mm(t.matmul(
                    dbc[:], self.EHN[:, ct4 * 128:(ct4 + 1) * 128], DELT[pu],
                    start=True, stop=True))
                s.wait(self.mm, mm_dbc)
                s.activation(ABAR[:, ct4 * TC:(ct4 + 1) * TC], dbc[:], AF.Exp,
                             scale=ACOL[:, ct4:ct4 + 1])
                sc_ab = s.tok
                pb = PAR_B[pu][:, ct4 * TC:(ct4 + 1) * TC]
                v.tensor_tensor(pb, pb, BF[:, ct4 * 2048 + u * TC:
                                           ct4 * 2048 + (u + 1) * TC], OP.mult)
                v.wait(self.sc, sc_ab)
                v.tensor_tensor(pb, pb, dbc[:], OP.mult)
                dbc_ev[ct4 % 2] = v.tok
                ab = ABAR[:, ct4 * TC:(ct4 + 1) * TC]
                hst = HST[:, ct4 * TC:(ct4 + 1) * TC]
                init = 0.0 if u == 0 else HCAR[:, ct4:ct4 + 1]
                v.tensor_tensor_scan(hst, ab, pb, init, OP.mult, OP.add)
                v.tensor_copy(HCAR[:, ct4:ct4 + 1], hst[:, TC - 1:TC])
                pc = PAR_C[pu][:, ct4 * TC:(ct4 + 1) * TC]
                v.tensor_tensor(pc, pc, hst, OP.mult)
                v.scalar_tensor_tensor(pc, BF[:, ct4 * 2048 + u * TC:
                                              ct4 * 2048 + (u + 1) * TC],
                                       DCOL2[:, ct4:ct4 + 1], pc, OP.mult,
                                       OP.add)
                v.tensor_tensor(YBF[:, ct4 * TC:(ct4 + 1) * TC], pc,
                                ZS[:, ct4 * 2048 + u * TC:
                                   ct4 * 2048 + (u + 1) * TC], OP.mult)
            o["ev_y"] = v.tok
            o["mm_dbc"] = mm_dbc
            for dblk in range(NK):
                slot = dblk % 2
                wsl = wo_dst[dblk]
                bank = self.ps[5 + slot]
                t.wait_ge(self.ev.h, o["ev_y"])
                if wo_ev[slot]:
                    t.wait_ge(self.ev.h, wo_ev[slot])
                for kc in range(4):
                    last = t.matmul(bank[:], wsl[:, kc * 128:(kc + 1) * 128],
                                    YBF[:, kc * TC:(kc + 1) * TC],
                                    start=(kc == 0), stop=(kc == 3))
                wo_mm[slot] = self._mm(last)
                rslot = self.RESG[:, 1024 + slot * TC: 1024 + (slot + 1) * TC]
                if wo_st[slot]:
                    self.nc.scalar.wait_ge(self.ssl[slot].h, wo_st[slot])
                rtok = self._dma(self.nc.scalar, self.rsl[slot], rslot,
                                 res_src[dblk * 128:(dblk + 1) * 128, ccs])
                v.wait(self.mm, wo_mm[slot])
                v.wait(self.rsl[slot], rtok)
                v.tensor_tensor(rslot, bank[:], rslot, OP.add)
                wo_ev[slot] = v.tok
                q.wait_ge(self.ev.h, wo_ev[slot])
                wo_st[slot] = self._dma(
                    q, self.ssl[slot],
                    res_dst[dblk * 128:(dblk + 1) * 128, ccs], rslot)
            o["wo_mm"] = wo_mm[slot]

        P1(0)
        P1(1)
        P1(2)
        P2(0)
        P1(3)
        P2(1)
        P2(2)
        P2(3)

    def barrier_all(self):
        sems = ([self.xin, self.wgrp, self.hld, self.xq, self.mm, self.ev,
                 self.sc, self.gp] + self.xint + self.wsl + self.rsl +
                self.ssl + self.hsl)
        for eng in (self.t, self.q, self.nc.vector, self.nc.scalar, self.nc.gpsimd):
            for sem in sems:
                if sem.n > 0:
                    eng.wait_ge(sem.h, sem.n)


def host_layout_l0ffn(w_gate_up, w_down):
    wgu = np.ascontiguousarray(
        w_gate_up.T.reshape(16, 128, 128, 128).transpose(2, 1, 0, 3))
    wdn = np.ascontiguousarray(
        w_down.T.reshape(64, 128, 16, 128).transpose(2, 1, 0, 3))
    return wgu, wdn


def host_layout_l0attn(B, C, Dv):
    bt = np.ascontiguousarray(B.T.reshape(16, 128, 8, 128).transpose(2, 1, 0, 3))
    ct = np.ascontiguousarray(C.T.reshape(8, 128, 16, 128).transpose(2, 1, 0, 3))
    dcol = np.ascontiguousarray(Dv.reshape(16, 128).T)
    return bt, ct, dcol


def host_layout_l1ffn(select_weight, select_mean_diff, imp_center, imp_slope,
                      lora_update, lora_degen, bias_w, debias_w):
    selw = np.ascontiguousarray(
        select_weight.transpose(1, 0, 2).reshape(16, 128, 64).transpose(1, 0, 2))
    mu = np.ascontiguousarray(
        select_mean_diff.T.reshape(16, 128, 8).transpose(1, 0, 2))
    upd = np.ascontiguousarray(
        lora_update.transpose(1, 0, 2).reshape(16, 128, 64).transpose(1, 0, 2))
    dgn = np.ascontiguousarray(lora_degen.transpose(0, 2, 1).reshape(64, 2048))
    bias = np.ascontiguousarray(bias_w)
    debias = np.ascontiguousarray(
        debias_w.T.reshape(16, 128, 8).transpose(1, 0, 2))
    censlo = np.ascontiguousarray(np.stack([imp_center, imp_slope], axis=1))
    return selw, mu, upd, dgn, bias, debias, censlo


def host_layout_l1attn(W_in_x, W_in_z, conv_w, conv_b, W_param, W_dt, b_dt,
                       A_log, Dv, W_out):
    wx = np.ascontiguousarray(
        W_in_x.T.reshape(16, 128, 4, 128).transpose(2, 1, 0, 3))
    wz = np.ascontiguousarray(
        W_in_z.T.reshape(16, 128, 4, 128).transpose(2, 1, 0, 3))
    wp = np.ascontiguousarray(
        W_param.T.reshape(4, 128, 9, 128).transpose(2, 1, 0, 3))
    wo = np.ascontiguousarray(
        W_out.T.reshape(4, 128, 16, 128).transpose(2, 1, 0, 3))
    wdt = np.ascontiguousarray(W_dt.T)
    convw = np.ascontiguousarray(
        conv_w.reshape(512, 4).reshape(4, 128, 4).transpose(1, 0, 2).reshape(128, 16))
    convb = np.ascontiguousarray(conv_b.reshape(4, 128).T)
    alog = np.ascontiguousarray(A_log.reshape(512).reshape(4, 128).T)
    dval = np.ascontiguousarray(Dv.reshape(4, 128).T)
    bdt = np.ascontiguousarray(b_dt.reshape(32, 1))
    return wx, wz, wp, wo, wdt, convw, convb, alog, dval, bdt


def build_l0ffn_only():
    nc = bass.Bass("TRN2", target_bir_lowering=False)
    res_in = nc.dram_tensor("res_in", [D, T], F32, kind="ExternalInput").ap()
    wgu = nc.dram_tensor("wgu", [128, 128, 16, 128], F32, kind="ExternalInput").ap()
    wdn = nc.dram_tensor("wdn", [16, 128, 64, 128], F32, kind="ExternalInput").ap()
    res_out = nc.dram_tensor("res_out", [D, T], F32, kind="ExternalOutput").ap()
    h_scr = nc.dram_tensor("h_scr", [64, 128, T], BF16, kind="Internal").ap()
    p = P(nc)
    p.emit_consts()
    p.emit_l0ffn(res_in, res_out, wgu, wdn, h_scr)
    p.barrier_all()
    return nc


def build_l0attn_only():
    nc = bass.Bass("TRN2", target_bir_lowering=False)
    res_in = nc.dram_tensor("res_in", [D, T], F32, kind="ExternalInput").ap()
    bt = nc.dram_tensor("bt", [8, 128, 16, 128], F32, kind="ExternalInput").ap()
    ct = nc.dram_tensor("ct", [16, 128, 8, 128], F32, kind="ExternalInput").ap()
    dcol = nc.dram_tensor("dcol", [128, 16], F32, kind="ExternalInput").ap()
    res_out = nc.dram_tensor("res_out", [D, T], F32, kind="ExternalOutput").ap()
    p = P(nc)
    p.emit_consts()
    p.emit_l0attn(res_in, res_out, bt, ct, dcol)
    p.barrier_all()
    return nc


def build_l1ffn_only():
    nc = bass.Bass("TRN2", target_bir_lowering=False)
    res_in = nc.dram_tensor("res_in", [D, T], F32, kind="ExternalInput").ap()
    selw = nc.dram_tensor("selw", [128, 16, 64], F32, kind="ExternalInput").ap()
    mu = nc.dram_tensor("mu", [128, 16, 8], F32, kind="ExternalInput").ap()
    upd = nc.dram_tensor("upd", [128, 16, 64], F32, kind="ExternalInput").ap()
    dgn = nc.dram_tensor("dgn", [64, 2048], F32, kind="ExternalInput").ap()
    bias = nc.dram_tensor("bias", [8, 2048], F32, kind="ExternalInput").ap()
    debias = nc.dram_tensor("debias", [128, 16, 8], F32, kind="ExternalInput").ap()
    censlo = nc.dram_tensor("censlo", [8, 2], F32, kind="ExternalInput").ap()
    res_out = nc.dram_tensor("res_out", [D, T], F32, kind="ExternalOutput").ap()
    p = P(nc)
    p.emit_consts()
    p.emit_l1ffn(res_in, res_out, selw, mu, upd, dgn, bias, debias, censlo, first=True)
    p.barrier_all()
    return nc


def build_l1attn_only():
    nc = bass.Bass("TRN2", target_bir_lowering=False)
    res_in = nc.dram_tensor("res_in", [D, T], F32, kind="ExternalInput").ap()
    wx = nc.dram_tensor("wx", [4, 128, 16, 128], F32, kind="ExternalInput").ap()
    wz = nc.dram_tensor("wz", [4, 128, 16, 128], F32, kind="ExternalInput").ap()
    wp = nc.dram_tensor("wp", [9, 128, 4, 128], F32, kind="ExternalInput").ap()
    wo = nc.dram_tensor("wo", [16, 128, 4, 128], F32, kind="ExternalInput").ap()
    wdt = nc.dram_tensor("wdt", [128, 32], F32, kind="ExternalInput").ap()
    convw = nc.dram_tensor("convw", [128, 16], F32, kind="ExternalInput").ap()
    convb = nc.dram_tensor("convb", [128, 4], F32, kind="ExternalInput").ap()
    alog = nc.dram_tensor("alog", [128, 4], F32, kind="ExternalInput").ap()
    dval = nc.dram_tensor("dval", [128, 4], F32, kind="ExternalInput").ap()
    bdt = nc.dram_tensor("bdt", [32, 1], F32, kind="ExternalInput").ap()
    res_out = nc.dram_tensor("res_out", [D, T], F32, kind="ExternalOutput").ap()
    p = P(nc)
    p.emit_consts()
    p.emit_l1attn(res_in, res_out, wx, wz, wp, wo, wdt, convw, convb,
                  alog, dval, bdt)
    p.barrier_all()
    return nc


_PROG_CACHE = {}


def _prog(name, builder):
    if name not in _PROG_CACHE:
        _PROG_CACHE[name] = builder()
    return _PROG_CACHE[name]


def kernel_unfused(**inputs):
    wgu, wdn = host_layout_l0ffn(np.asarray(inputs["L0_ffn_w_gate_up"]),
                                 np.asarray(inputs["L0_ffn_w_down"]))
    bt, ct, dcol = host_layout_l0attn(np.asarray(inputs["L0_attn_B"]),
                                      np.asarray(inputs["L0_attn_C"]),
                                      np.asarray(inputs["L0_attn_D"]))
    selw, mu, upd, dgn, bias, debias, censlo = host_layout_l1ffn(
        np.asarray(inputs["L1_ffn_select_weight"]),
        np.asarray(inputs["L1_ffn_select_mean_diff"]),
        np.asarray(inputs["L1_ffn_imp_center"]),
        np.asarray(inputs["L1_ffn_imp_slope"]),
        np.asarray(inputs["L1_ffn_lora_update"]),
        np.asarray(inputs["L1_ffn_lora_degen"]),
        np.asarray(inputs["L1_ffn_bias_w"]),
        np.asarray(inputs["L1_ffn_debias_w"]))
    wx, wz, wp, wo, wdt, convw, convb, alog, dval, bdt = host_layout_l1attn(
        np.asarray(inputs["L1_attn_W_in_x"]), np.asarray(inputs["L1_attn_W_in_z"]),
        np.asarray(inputs["L1_attn_conv_w"]), np.asarray(inputs["L1_attn_conv_b"]),
        np.asarray(inputs["L1_attn_W_param"]), np.asarray(inputs["L1_attn_W_dt"]),
        np.asarray(inputs["L1_attn_b_dt"]), np.asarray(inputs["L1_attn_A_log"]),
        np.asarray(inputs["L1_attn_D"]), np.asarray(inputs["L1_attn_W_out"]))
    x = np.asarray(inputs["L0_attn_x"])
    res = [np.ascontiguousarray(x[b].T) for b in range(8)]

    launches = [
        ("l0ffn", build_l0ffn_only, {"wgu": wgu, "wdn": wdn}),
        ("l0attn", build_l0attn_only, {"bt": bt, "ct": ct, "dcol": dcol}),
        ("l0ffn", build_l0ffn_only, {"wgu": wgu, "wdn": wdn}),
        ("l1ffn", build_l1ffn_only, {"selw": selw, "mu": mu, "upd": upd,
                                     "dgn": dgn, "bias": bias, "debias": debias,
                                     "censlo": censlo}),
        ("l1attn", build_l1attn_only, {"wx": wx, "wz": wz, "wp": wp, "wo": wo,
                                       "wdt": wdt, "convw": convw, "convb": convb,
                                       "alog": alog, "dval": dval, "bdt": bdt}),
        ("l1ffn", build_l1ffn_only, {"selw": selw, "mu": mu, "upd": upd,
                                     "dgn": dgn, "bias": bias, "debias": debias,
                                     "censlo": censlo}),
    ]
    for name, builder, wmap in launches:
        nc = _prog(name, builder)
        in_maps = [dict(wmap, res_in=res[b]) for b in range(8)]
        out = run_bass_kernel_spmd(nc, in_maps, core_ids=list(range(8)))
        res = [out.results[b]["res_out"] for b in range(8)]
    return np.stack([r.T for r in res]).astype(np.float32)


def build_fused():
    nc = bass.Bass("TRN2", target_bir_lowering=False)
    DT = nc.dram_tensor
    res_in = DT("res_in", [D, T], F32, kind="ExternalInput").ap()
    wgu = DT("wgu", [128, 128, 16, 128], F32, kind="ExternalInput").ap()
    wdn = DT("wdn", [16, 128, 64, 128], F32, kind="ExternalInput").ap()
    bt = DT("bt", [8, 128, 16, 128], F32, kind="ExternalInput").ap()
    ct = DT("ct", [16, 128, 8, 128], F32, kind="ExternalInput").ap()
    dcol = DT("dcol", [128, 16], F32, kind="ExternalInput").ap()
    selw = DT("selw", [128, 16, 64], F32, kind="ExternalInput").ap()
    mu = DT("mu", [128, 16, 8], F32, kind="ExternalInput").ap()
    upd = DT("upd", [128, 16, 64], F32, kind="ExternalInput").ap()
    dgn = DT("dgn", [64, 2048], F32, kind="ExternalInput").ap()
    bias = DT("bias", [8, 2048], F32, kind="ExternalInput").ap()
    debias = DT("debias", [128, 16, 8], F32, kind="ExternalInput").ap()
    censlo = DT("censlo", [8, 2], F32, kind="ExternalInput").ap()
    wx = DT("wx", [4, 128, 16, 128], F32, kind="ExternalInput").ap()
    wz = DT("wz", [4, 128, 16, 128], F32, kind="ExternalInput").ap()
    wp = DT("wp", [9, 128, 4, 128], F32, kind="ExternalInput").ap()
    wo = DT("wo", [16, 128, 4, 128], F32, kind="ExternalInput").ap()
    wdt = DT("wdt", [128, 32], F32, kind="ExternalInput").ap()
    convw = DT("convw", [128, 16], F32, kind="ExternalInput").ap()
    convb = DT("convb", [128, 4], F32, kind="ExternalInput").ap()
    alog = DT("alog", [128, 4], F32, kind="ExternalInput").ap()
    dval = DT("dval", [128, 4], F32, kind="ExternalInput").ap()
    bdt = DT("bdt", [32, 1], F32, kind="ExternalInput").ap()
    res_out = DT("res_out", [D, T], F32, kind="ExternalOutput").ap()
    ra = DT("res_a", [D, T], F32, kind="Internal").ap()
    rb = DT("res_b", [D, T], F32, kind="Internal").ap()
    h_scr = DT("h_scr", [64, 128, T], BF16, kind="Internal").ap()
    p = P(nc)
    p.emit_consts()
    p.emit_l0ffn(res_in, ra, wgu, wdn, h_scr)
    p.barrier_all()
    p.emit_l0attn(ra, rb, bt, ct, dcol)
    p.barrier_all()
    p.emit_l0ffn(rb, ra, wgu, wdn, h_scr)
    p.barrier_all()
    p.emit_l1ffn(ra, rb, selw, mu, upd, dgn, bias, debias, censlo, first=True)
    p.barrier_all()
    p.emit_l1attn(rb, ra, wx, wz, wp, wo, wdt, convw, convb, alog, dval, bdt)
    p.barrier_all()
    p.emit_l1ffn(ra, res_out, selw, mu, upd, dgn, bias, debias, censlo, first=False)
    p.barrier_all()
    return nc


def _host_weight_maps(inputs):
    wgu, wdn = host_layout_l0ffn(np.asarray(inputs["L0_ffn_w_gate_up"]),
                                 np.asarray(inputs["L0_ffn_w_down"]))
    bt, ct, dcol = host_layout_l0attn(np.asarray(inputs["L0_attn_B"]),
                                      np.asarray(inputs["L0_attn_C"]),
                                      np.asarray(inputs["L0_attn_D"]))
    selw, mu, upd, dgn, bias, debias, censlo = host_layout_l1ffn(
        np.asarray(inputs["L1_ffn_select_weight"]),
        np.asarray(inputs["L1_ffn_select_mean_diff"]),
        np.asarray(inputs["L1_ffn_imp_center"]),
        np.asarray(inputs["L1_ffn_imp_slope"]),
        np.asarray(inputs["L1_ffn_lora_update"]),
        np.asarray(inputs["L1_ffn_lora_degen"]),
        np.asarray(inputs["L1_ffn_bias_w"]),
        np.asarray(inputs["L1_ffn_debias_w"]))
    wx, wz, wp, wo, wdt, convw, convb, alog, dval, bdt = host_layout_l1attn(
        np.asarray(inputs["L1_attn_W_in_x"]), np.asarray(inputs["L1_attn_W_in_z"]),
        np.asarray(inputs["L1_attn_conv_w"]), np.asarray(inputs["L1_attn_conv_b"]),
        np.asarray(inputs["L1_attn_W_param"]), np.asarray(inputs["L1_attn_W_dt"]),
        np.asarray(inputs["L1_attn_b_dt"]), np.asarray(inputs["L1_attn_A_log"]),
        np.asarray(inputs["L1_attn_D"]), np.asarray(inputs["L1_attn_W_out"]))
    return {"wgu": wgu, "wdn": wdn, "bt": bt, "ct": ct, "dcol": dcol,
            "selw": selw, "mu": mu, "upd": upd, "dgn": dgn, "bias": bias,
            "debias": debias, "censlo": censlo, "wx": wx, "wz": wz, "wp": wp,
            "wo": wo, "wdt": wdt, "convw": convw, "convb": convb, "alog": alog,
            "dval": dval, "bdt": bdt}


def kernel(**inputs):
    wmap = _host_weight_maps(inputs)
    x = np.asarray(inputs["L0_attn_x"])
    nc = _prog("fused", build_fused)
    in_maps = [dict(wmap, res_in=np.ascontiguousarray(x[b].T)) for b in range(8)]
    out = run_bass_kernel_spmd(nc, in_maps, core_ids=list(range(8)))
    return np.stack([out.results[b]["res_out"].T for b in range(8)]).astype(np.float32)


# version 13
# speedup vs baseline: 1.1222x; 1.1222x over previous
import numpy as np
import ml_dtypes
import concourse.bass as bass
import concourse.mybir as mybir
from concourse.alu_op_type import AluOpType as OP
from concourse.bass_utils import run_bass_kernel_spmd

F32 = mybir.dt.float32
BF16 = mybir.dt.bfloat16
I32 = mybir.dt.int32
AF = mybir.ActivationFunctionType

D = 2048
T = 2048
NK = 16
TC = 512
NTC = 4
EPS = 1e-5
NIB = 64

_CHAINED = {
    "tensor_tensor", "tensor_scalar", "tensor_single_scalar",
    "scalar_tensor_tensor", "tensor_copy", "tensor_reduce", "reduce_max",
    "reduce_sum", "reciprocal", "memset", "memzero", "iota", "activation",
    "copy", "mul", "add", "square", "sqrt", "sign", "tensor_tensor_scan",
    "select", "tensor_add", "tensor_mul", "tensor_sub", "tensor_max",
    "tensor_relu", "max_index", "max_with_indices", "bn_stats", "bn_aggr",
    "copy_predicated", "stream_shuffle", "tensor_scalar_add",
    "tensor_scalar_mul", "tensor_scalar_max", "tensor_scalar_min",
    "tensor_scalar_sub",
}


class Sem:
    def __init__(self, nc, name):
        self.h = nc.alloc_semaphore(name)
        self.n = 0


class Chained:

    def __init__(self, eng, sem):
        self._e = eng
        self._s = sem

    def __getattr__(self, name):
        attr = getattr(self._e, name)
        if name not in _CHAINED:
            return attr
        sem = self._s
        eng = self._e

        def wrapped(*a, **k):
            if sem.n > 0:
                eng.wait_ge(sem.h, sem.n)
            instr = attr(*a, **k)
            instr.then_inc(sem.h, 1)
            sem.n += 1
            return instr

        return wrapped

    @property
    def tok(self):
        return self._s.n

    def wait(self, sem, val):
        if val > 0:
            self._e.wait_ge(sem.h, val)


class P:

    def __init__(self, nc):
        self.nc = nc
        self.t = nc.tensor
        self.mm = Sem(nc, "mm")
        self.ev = Sem(nc, "ev")
        self.sc = Sem(nc, "sc")
        self.gp = Sem(nc, "gp")
        self.v = Chained(nc.vector, self.ev)
        self.s = Chained(nc.scalar, self.sc)
        self.g = Chained(nc.gpsimd, self.gp)
        self.q = nc.sync
        self.xin = Sem(nc, "xin")
        self.xint = [Sem(nc, f"xint{i}") for i in range(4)]
        self.wgrp = Sem(nc, "wgrp")
        self.hld = Sem(nc, "hld")
        self.xq = Sem(nc, "xq")
        self.wsl = [Sem(nc, f"wsl{i}") for i in range(6)]
        self.rsl = [Sem(nc, f"rsl{i}") for i in range(4)]
        self.rslg = [Sem(nc, f"rslg{i}") for i in range(2)]
        self.ssl = [Sem(nc, f"ssl{i}") for i in range(4)]
        self.hsl = [Sem(nc, f"hsl{i}") for i in range(2)]
        self.ps = [nc.alloc_psum_tensor(f"ps{i}", [128, 512], F32) for i in range(8)]
        A = nc.alloc_sbuf_tensor
        self.XBF = A("XBF", [128, NK * T], BF16)
        self.BIGB = A("BIGB", [128, 32768], BF16)
        self.WBUF = A("WBUF", [128, 8192], BF16)
        self.TMPF = A("TMPF", [128, 2048], F32)
        self.RESG = A("RESG", [128, 2048], F32)
        self.HEV = A("HEV", [128, 4096], BF16)
        self.MROW = A("MROW", [1, 2048], F32)
        self.RROW = A("RROW", [1, 2048], F32)
        self.NROW = A("NROW", [1, 2048], F32)
        self.RARE = A("RARE", [1, 2048], F32)
        self.SM = A("SM", [128, 1024], F32)
        self.ONEB = A("ONEB", [128, 1], BF16)
        self.ONE1 = A("ONE1", [1, 128], F32)
        self.ONEF = A("ONEF", [128, 1], F32)
        self.ONE8 = A("ONE8", [128, 8], F32)
        self.EPSC = A("EPSC", [128, 1], F32)
        self.HALFC = A("HALFC", [128, 1], F32)
        self.EHN = A("EHN", [32, 512], F32)
        self.ID128 = A("ID128", [128, 128], F32)
        self.CON = A("CON", [128, 128], F32)
        self.ESUM = self.CON[0:64, 52:60]
        self.E2F = self.CON[0:8, 64:128]
        self.XBF_F = self.XBF.ap().bitcast(F32)
        self.BIG_F = self.BIGB.ap().bitcast(F32)

    def _mm(self, last_instr):
        last_instr.then_inc(self.mm.h, 1)
        self.mm.n += 1
        return self.mm.n

    def _dma(self, eng, sem, out, in_):
        sem.n += 16
        eng.dma_start(out=out, in_=in_).then_inc(sem.h, 16)
        return sem.n

    def emit_consts(self):
        g = self.g
        g.memset(self.ONEB[:], 1.0)
        g.memset(self.ONE1[:], 1.0)
        g.memset(self.ONEF[:], 1.0)
        g.memset(self.ONE8[:], 1.0)
        g.memset(self.EPSC[:], EPS)
        g.memset(self.HALFC[:], 0.5)
        it32 = self.SM.ap().bitcast(I32)
        g.iota(it32[0:32, 0:512], [[0, 1], [0, 1], [-1, 32], [0, 16]],
               base=0, channel_multiplier=1)
        g.iota(it32[:, 512:640], [[-1, 128]], base=0, channel_multiplier=1)
        g.iota(it32[0:8, 648:712], [[0, 1], [0, 1], [-1, 8], [0, 8]],
               base=0, channel_multiplier=1)
        g.tensor_single_scalar(self.EHN[:], it32[0:32, 0:512], 0.0, OP.is_equal)
        g.tensor_single_scalar(self.ID128[:], it32[:, 512:640], 0.0, OP.is_equal)
        g.tensor_single_scalar(self.E2F, it32[0:8, 648:712], 0.0, OP.is_equal)
        self.con_done = self.gp.n
        self.t.wait_ge(self.gp.h, self.con_done)
        mm_es = self._mm(self.t.transpose(self.ps[0][0:64, 0:8], self.E2F,
                                          self.ID128[0:8, 0:8]))
        self.v.wait(self.mm, mm_es)
        self.v.tensor_copy(self.ESUM, self.ps[0][0:64, 0:8])
        self.esum_ev = self.v.tok
        self.t.wait_ge(self.ev.h, self.esum_ev)
        self.q.wait_ge(self.gp.h, self.con_done)
        self.s.wait(self.gp, self.con_done)

    def emit_ln(self, res_src, need_rn=False):
        t, v, s, g = self.t, self.v, self.s, self.g
        XBF = self.XBF
        ONES = self.HEV[:, 2048:2176]
        xsq = self.HEV.ap()
        NRS = self.BIG_F[0:8, 11008:13056]
        xv = XBF.ap().rearrange("p (a c) -> p a c", a=NK)
        self.ln_ev_tc = [0] * NTC
        g.memset(ONES, 1.0)
        gp_ones = self.gp.n
        t.wait_ge(self.gp.h, gp_ones)
        ld_tok = []
        for tc in range(NTC):
            cs = slice(tc * TC, (tc + 1) * TC)
            for k0 in (0, 4, 8, 12):
                self._dma(g, self.xint[tc], xv[:, k0:k0 + 4, cs],
                          res_src[k0 * 128:(k0 + 4) * 128, cs].rearrange(
                              "(a p) c -> p a c", p=128))
            ld_tok.append(self.xint[tc].n)
        sq_tok = []
        prev_ev = [0, 0]
        ev_norm_prev = 0
        for tc in range(NTC):
            cs = slice(tc * TC, (tc + 1) * TC)
            p = tc % 2
            ps_a, ps_b = self.ps[4 + p], self.ps[6 + p]
            MT = self.TMPF[:, p * TC:(p + 1) * TC]
            VT = self.TMPF[:, 1024 + p * TC: 1024 + (p + 1) * TC]
            s.wait(self.xint[tc], ld_tok[tc])
            t.wait_ge(self.xint[tc].h, ld_tok[tc])
            if tc >= 2:
                t.wait_ge(self.ev.h, prev_ev[p])
            for k in range(NK):
                it = tc * NK + k
                sl = xsq[:, (it % 2) * TC:(it % 2 + 1) * TC]
                xsl = XBF[:, k * T + tc * TC: k * T + (tc + 1) * TC]
                if it >= 2:
                    s.wait(self.mm, sq_tok[it - 2])
                s.activation(sl, xsl, AF.Square)
                sq_sc = s.tok
                t.matmul(ps_a[:], ONES, xsl, start=(k == 0), stop=(k == NK - 1))
                t.wait_ge(self.sc.h, sq_sc)
                last = t.matmul(ps_b[:], ONES, sl, start=(k == 0),
                                stop=(k == NK - 1))
                sq_tok.append(self._mm(last))
            mm_st = self.mm.n
            self.ln_mm_stats = mm_st
            v.wait(self.mm, mm_st)
            v.tensor_single_scalar(MT, ps_a[:], 1.0 / D, OP.mult)
            v.tensor_tensor(VT, MT, MT, OP.mult)
            v.scalar_tensor_tensor(VT, ps_b[:], 1.0 / D, VT, OP.mult,
                                   OP.subtract)
            if need_rn:
                v.tensor_copy(self.MROW[:, cs], MT[0:1, :])
                v.tensor_copy(NRS[:, cs], VT[0:8, :])
            ev_v = v.tok
            prev_ev[p] = ev_v
            self.ln_ev_stats = ev_v
            s.wait(self.ev, ev_v)
            s.activation(VT, VT, AF.Ln, bias=self.EPSC[:, :])
            s.activation(VT, VT, AF.Exp, scale=-0.5)
            sc_r = s.tok
            v.wait(self.sc, sc_r)
            if need_rn:
                v.tensor_copy(self.RROW[:, cs], VT[0:1, :])
                v.tensor_tensor(NRS[:, cs], NRS[:, cs], VT[0:8, :], OP.mult)
                v.tensor_tensor(NRS[:, cs], NRS[:, cs], VT[0:8, :], OP.mult)
                v.tensor_single_scalar(NRS[:, cs], NRS[:, cs], float(D),
                                       OP.mult)
            TT = self.RESG[:, 0:TC]
            for k in range(NK):
                xsl = XBF[:, k * T + tc * TC: k * T + (tc + 1) * TC]
                v.tensor_tensor(TT, xsl, MT, OP.subtract)
                v.tensor_tensor(xsl, TT, VT, OP.mult)
            ev_norm_prev = v.tok
            self.ln_ev_tc[tc] = ev_norm_prev
        self.ln_done = ev_norm_prev

    def emit_l0ffn(self, res_src, res_dst, wgu, wdn, h_scr):
        t, v, s, g, q = self.t, self.v, self.s, self.g, self.q
        self.emit_ln(res_src)
        XBF, WBUF, TMPF, HEV = self.XBF, self.WBUF, self.TMPF, self.HEV
        big = self.BIGB
        mm_tok = [0] * 128
        ev_tok = [0] * 64
        sc_tok = [0] * 64
        st_tok = [0] * 64
        for jp in range(64):
            for half, j in ((0, jp), (1, jp + 64)):
                idx = 2 * jp + half
                slot = idx % 4
                wsl = WBUF[:, slot * 2048:(slot + 1) * 2048]
                if idx >= 4:
                    g.wait_ge(self.mm.h, mm_tok[idx - 4])
                tok = self._dma(g, self.wsl[slot], wsl,
                                wgu[j].rearrange("p k c -> p (k c)"))
                banks = self.ps[0:4] if half == 0 else self.ps[4:8]
                t.wait_ge(self.wsl[slot].h, tok)
                if jp > 0:
                    if half == 0:
                        t.wait_ge(self.sc.h, sc_tok[jp - 1])
                    else:
                        t.wait_ge(self.ev.h, ev_tok[jp - 1])
                if idx == 1:
                    t.wait_ge(self.ev.h, self.ln_ev_stats)
                for tc in range(NTC):
                    if idx < 2:
                        t.wait_ge(self.ev.h, self.ln_ev_tc[tc])
                    for k in range(NK):
                        last = t.matmul(banks[tc][:], wsl[:, k * 128:(k + 1) * 128],
                                        XBF[:, k * T + tc * TC: k * T + (tc + 1) * TC],
                                        start=(k == 0), stop=(k == NK - 1))
                mm_tok[idx] = self._mm(last)
            s.wait(self.mm, mm_tok[2 * jp])
            if jp > 0:
                s.wait(self.ev, ev_tok[jp - 1])
            for tc in range(NTC):
                s.activation(TMPF[:, tc * TC:(tc + 1) * TC], self.ps[tc][:], AF.Silu)
            sc_tok[jp] = s.tok
            hslot = HEV[:, (jp % 2) * 2048:(jp % 2 + 1) * 2048]
            v.wait(self.mm, mm_tok[2 * jp + 1])
            v.wait(self.sc, sc_tok[jp])
            if jp >= 2:
                v.wait(self.hsl[jp % 2], st_tok[jp - 2])
            for tc in range(NTC):
                v.tensor_tensor(hslot[:, tc * TC:(tc + 1) * TC],
                                TMPF[:, tc * TC:(tc + 1) * TC],
                                self.ps[4 + tc][:], OP.mult)
            ev_tok[jp] = v.tok
            q.wait_ge(self.ev.h, ev_tok[jp])
            st_tok[jp] = self._dma(q, self.hsl[jp % 2], h_scr[jp], hslot)
        g.wait_ge(self.mm.h, mm_tok[125])
        wd_pre0 = self._dma(g, self.wsl[0], WBUF[:, 0:4096],
                            wdn[0, :, 32:64, :].rearrange("p i c -> p (i c)"))
        g.wait_ge(self.mm.h, mm_tok[127])
        wd_pre1 = self._dma(g, self.wsl[1], WBUF[:, 4096:8192],
                            wdn[0, :, 0:32, :].rearrange("p i c -> p (i c)"))
        for ib in range(32, NIB):
            g.wait_ge(self.hsl[ib % 2].h, st_tok[ib])
            self._dma(g, self.hld, big[:, (ib - 32) * 1024:(ib - 31) * 1024],
                      h_scr[ib][:, 0:1024])
        h_stored = (self.hsl[0].n, self.hsl[1].n)
        dn_ev = [0, 0]
        dn_st = [0, 0]
        wd_mm = [0, 0]
        for tc2 in range(2):
            cs2 = slice(tc2 * 1024, (tc2 + 1) * 1024)
            q.wait_ge(self.hsl[0].h, h_stored[0])
            q.wait_ge(self.hsl[1].h, h_stored[1])
            q.wait_ge(self.mm.h, mm_tok[127])
            self.nc.scalar.wait_ge(self.hsl[0].h, h_stored[0])
            self.nc.scalar.wait_ge(self.hsl[1].h, h_stored[1])
            self.nc.scalar.wait_ge(self.mm.h, mm_tok[127])
            if tc2 > 0:
                q.wait_ge(self.ev.h, self.ev.n)
                self.nc.scalar.wait_ge(self.ev.h, self.ev.n)
            for ib in range(NIB):
                dst = (XBF[:, ib * 1024:(ib + 1) * 1024] if ib < 32
                       else big[:, (ib - 32) * 1024:(ib - 31) * 1024])
                if tc2 == 0 and ib >= 32:
                    continue
                eng = q if ib % 2 == 0 else self.nc.scalar
                self._dma(eng, self.xq, dst, h_scr[ib][:, cs2])
            h_ready = self.xq.n
            t.wait_ge(self.hld.h, self.hld.n)
            if tc2 > 0:
                t.wait_ge(self.xq.h, h_ready)
            for dblk in range(NK):
                b0 = self.ps[2 * (dblk % 2)]
                b1 = self.ps[2 * (dblk % 2) + 1]
                for pos_h, hh in enumerate((1, 0)):
                    pos = 2 * dblk + pos_h
                    slot = pos % 2
                    wsl = WBUF[:, slot * 4096:(slot + 1) * 4096]
                    if tc2 == 0 and pos == 0:
                        tok = wd_pre0
                    elif tc2 == 0 and pos == 1:
                        tok = wd_pre1
                    else:
                        g.wait_ge(self.mm.h, wd_mm[slot])
                        tok = self._dma(g, self.wsl[slot], wsl,
                                        wdn[dblk, :, hh * 32:(hh + 1) * 32,
                                            :].rearrange("p i c -> p (i c)"))
                    t.wait_ge(self.wsl[slot].h, tok)
                    if pos_h == 0 and (dblk >= 2 or tc2 > 0):
                        t.wait_ge(self.ev.h, dn_ev[dblk % 2])
                    if tc2 == 0 and pos == 1:
                        t.wait_ge(self.xq.h, h_ready)
                    for ib in range(32):
                        gib = hh * 32 + ib
                        hsrc = (XBF[:, gib * 1024:(gib + 1) * 1024] if gib < 32
                                else big[:, (gib - 32) * 1024:(gib - 31) * 1024])
                        st_ = (pos_h == 0 and ib == 0)
                        sp_ = (pos_h == 1 and ib == 31)
                        t.matmul(b0[:], wsl[:, ib * 128:(ib + 1) * 128],
                                 hsrc[:, 0:512], start=st_, stop=sp_)
                        last = t.matmul(b1[:], wsl[:, ib * 128:(ib + 1) * 128],
                                        hsrc[:, 512:1024], start=st_, stop=sp_)
                    wd_mm[slot] = self._mm(last)
                mm_done = wd_mm[1]
                rslot = self.RESG[:, (dblk % 2) * 1024:(dblk % 2 + 1) * 1024]
                if dblk >= 2 or tc2 > 0:
                    self.nc.scalar.wait_ge(self.ssl[dblk % 2].h, dn_st[dblk % 2])
                rtok = self._dma(self.nc.scalar, self.rsl[dblk % 2], rslot,
                                 res_src[dblk * 128:(dblk + 1) * 128, cs2])
                v.wait(self.mm, mm_done)
                v.wait(self.rsl[dblk % 2], rtok)
                v.scalar_tensor_tensor(rslot[:, 0:512], b0[:], 0.5, rslot[:, 0:512],
                                       OP.mult, OP.add)
                v.scalar_tensor_tensor(rslot[:, 512:1024], b1[:], 0.5,
                                       rslot[:, 512:1024], OP.mult, OP.add)
                dn_ev[dblk % 2] = v.tok
                q.wait_ge(self.ev.h, dn_ev[dblk % 2])
                dn_st[dblk % 2] = self._dma(
                    q, self.ssl[dblk % 2],
                    res_dst[dblk * 128:(dblk + 1) * 128, cs2], rslot)

    def emit_l0attn(self, res_src, res_dst, bt, ct, dcol):
        t, v, s, g, q = self.t, self.v, self.s, self.g, self.q
        self.emit_ln(res_src)
        XBF, WBUF = self.XBF, self.WBUF
        STATE = self.BIGB
        dc = self._dma(q, self.xq, self.CON[:, 0:16], dcol)
        DIAG = self.HEV
        g.wait_ge(self.xq.h, dc)
        g.wait_ge(self.mm.h, self.mm.n)
        for dblk in range(NK):
            g.tensor_scalar(DIAG[:, dblk * 128:(dblk + 1) * 128], self.ID128[:],
                            self.CON[:, dblk:dblk + 1], None, OP.mult)
        gp_diag = self.gp.n
        t.wait_ge(self.gp.h, gp_diag)
        mm_tok = [0] * 8
        sc_tok = [0] * 8
        for nb in range(8):
            slot = nb % 3
            wsl = WBUF[:, slot * 2048: slot * 2048 + 2048]
            if nb >= 3:
                g.wait_ge(self.mm.h, mm_tok[nb - 3])
            tok = self._dma(g, self.wsl[slot], wsl,
                            bt[nb].rearrange("p k c -> p (k c)"))
            banks = self.ps[0:4] if nb % 2 == 0 else self.ps[4:8]
            t.wait_ge(self.wsl[slot].h, tok)
            if nb >= 2:
                t.wait_ge(self.sc.h, sc_tok[nb - 2])
            if nb == 1:
                t.wait_ge(self.ev.h, self.ln_ev_stats)
            for tc in range(NTC):
                if nb < 2:
                    t.wait_ge(self.ev.h, self.ln_ev_tc[tc])
                for k in range(NK):
                    last = t.matmul(banks[tc][:], wsl[:, k * 128:(k + 1) * 128],
                                    XBF[:, k * T + tc * TC: k * T + (tc + 1) * TC],
                                    start=(k == 0), stop=(k == NK - 1))
            mm_tok[nb] = self._mm(last)
            s.wait(self.mm, mm_tok[nb])
            for tc in range(NTC):
                s.activation(STATE[:, nb * T + tc * TC: nb * T + (tc + 1) * TC],
                             banks[tc][:], AF.Tanh)
            sc_tok[nb] = s.tok
        dn_ev = [0, 0]
        dn_st4 = [0, 0, 0, 0]
        ct_mm = [0, 0]
        v.wait(self.xq, dc)
        for dblk in range(NK):
            slot = dblk % 2
            wsl = WBUF[:, 6144 + slot * 1024: 6144 + slot * 1024 + 1024]
            if dblk >= 2:
                g.wait_ge(self.mm.h, ct_mm[slot])
            tok = self._dma(g, self.wsl[4 + slot], wsl,
                            ct[dblk].rearrange("p n c -> p (n c)"))
            banks = self.ps[0:4] if dblk % 2 == 0 else self.ps[4:8]
            t.wait_ge(self.wsl[4 + slot].h, tok)
            if dblk >= 2:
                t.wait_ge(self.ev.h, dn_ev[slot])
            if dblk < 2:
                t.wait_ge(self.sc.h, sc_tok[7])
            for nb in range(8):
                for tc in range(NTC):
                    t.matmul(banks[tc][:], wsl[:, nb * 128:(nb + 1) * 128],
                             STATE[:, nb * T + tc * TC: nb * T + (tc + 1) * TC],
                             start=(nb == 0), stop=False)
            for tc in range(NTC):
                last = t.matmul(banks[tc][:], DIAG[:, dblk * 128:(dblk + 1) * 128],
                                XBF[:, dblk * T + tc * TC: dblk * T + (tc + 1) * TC],
                                start=False, stop=True)
            ct_mm[slot] = self._mm(last)
            for tc in range(NTC):
                rslot = self.RESG[:, tc * TC:(tc + 1) * TC]
                if dblk >= 1:
                    self.nc.scalar.wait_ge(self.ssl[tc].h, dn_st4[tc])
                rtok = self._dma(self.nc.scalar, self.rsl[tc], rslot,
                                 res_src[dblk * 128:(dblk + 1) * 128,
                                         tc * TC:(tc + 1) * TC])
                v.wait(self.mm, ct_mm[slot])
                v.wait(self.rsl[tc], rtok)
                v.tensor_tensor(rslot, banks[tc][:], rslot, OP.add)
                dn_ev[slot] = v.tok
                q.wait_ge(self.ev.h, dn_ev[slot])
                dn_st4[tc] = self._dma(
                    q, self.ssl[tc],
                    res_dst[dblk * 128:(dblk + 1) * 128, tc * TC:(tc + 1) * TC],
                    rslot)

    def emit_l1ffn(self, res_src, res_dst, selw, mu, upd, dgn, bias, debias,
                   censlo, first=True):
        t, v, s, g, q = self.t, self.v, self.s, self.g, self.q
        self.emit_ln(res_src, need_rn=True)
        XBF, WBUF, CON, BF = self.XBF, self.WBUF, self.CON, self.BIG_F
        SELW = WBUF[:, 0:1024]
        UPD = WBUF[:, 1024:2048]
        MUB = WBUF[:, 2048:2176]
        DGN = WBUF[0:64, 2560:4608]
        BIA = WBUF[0:8, 4608:6656]
        V2F = [BF[0:64, 0:512], BF[0:64, 512:1024]]
        MODX = [BF[0:64, 1024:1536], BF[0:64, 1536:2048]]
        QSF = [BF[0:8, 3072:3584], BF[0:8, 3584:4096]]
        DEN = [BF[0:8, 4096:4608], BF[0:8, 4608:5120]]
        GF = [BF[0:8, 5120:5632], BF[0:8, 5632:6144]]
        FMS = [BF[0:8, 6144:6656], BF[0:8, 6656:7168]]
        FGF = [BF[0:8, 7168:7680], BF[0:8, 7680:8192]]
        MMX = [self.HEV[0:64, 1024:1536], self.HEV[0:64, 1536:2048]]
        FGB = [WBUF[0:8, 6656:7168], WBUF[0:8, 7168:7680]]
        SMB = self.SM.ap().bitcast(BF16)
        MCB = [SMB[0:1, 0:512], SMB[0:1, 512:1024]]
        ONE1B = SMB[0:1, 1024:1152]
        TMJ = [BF[:, 8192:8224], BF[:, 8224:8256]]
        IMW = [BF[:, 8256:8288], BF[:, 8288:8320]]
        PSW = [BF[:, 8320:8352], BF[:, 8352:8384]]
        FMW = [BF[:, 8384:8416], BF[:, 8416:8448]]
        MX = [BF[:, 8448:8452], BF[:, 8452:8456]]
        RC = [BF[0:1, 8456:8968], BF[0:1, 8968:9480]]
        MC = [BF[0:1, 9480:9992], BF[0:1, 9992:10504]]
        MUF = WBUF[:, 2176:2304]
        DBT = WBUF[:, 2304:2432]
        MSQ = BF[:, 10760:10888]
        NEGCT = BF[0:64, 10888:10896]
        T81 = BF[0:8, 10896:10897]
        NEGC = CON[0:64, 45:46]
        NEGC2 = CON[0:64, 46:47]
        M2C = CON[0:8, 47:48]
        GBIAS = CON[0:8, 49:50]
        CENT = CON[0:8, 50:51]
        SLOP = CON[0:8, 51:52]
        g.memset(ONE1B, 1.0)
        gp_one1b = self.gp.n
        for dst, src in ((SELW, selw.rearrange("p a b -> p (a b)")),
                         (UPD, upd.rearrange("p a b -> p (a b)")),
                         (MUB, mu.rearrange("p a b -> p (a b)")),
                         (DGN, dgn), (BIA, bias)):
            self._dma(g, self.wgrp, dst, src)
        t.wait_ge(self.gp.h, gp_one1b)
        if first:
            for dst, src in ((MUF, mu.rearrange("p a b -> p (a b)")),
                             (DBT, debias.rearrange("p a b -> p (a b)"))):
                self._dma(g, self.wgrp, dst, src)
            self._dma(q, self.xq, CON[0:8, 50:52], censlo)
        t.wait_ge(self.wgrp.h, self.wgrp.n)
        if first:
            xld = self.xq.n
            s.wait(self.xq, xld)
            s.wait(self.wgrp, self.wgrp.n)
            s.activation(MSQ, MUF, AF.Square)
            sc_msq = s.tok
            t.wait_ge(self.sc.h, sc_msq)
            t.wait_ge(self.ev.h, self.ln_ev_stats)
            for k in range(NK):
                t.matmul(self.ps[4][0:8, 0:1], MSQ[:, k * 8:(k + 1) * 8],
                         self.ONEF[:], start=(k == 0), stop=(k == NK - 1))
                t.matmul(self.ps[5][0:64, 0:8], SELW[:, k * 64:(k + 1) * 64],
                         MUF[:, k * 8:(k + 1) * 8],
                         start=(k == 0), stop=(k == NK - 1))
                last = t.matmul(self.ps[6][0:64, 0:8],
                                UPD[:, k * 64:(k + 1) * 64],
                                DBT[:, k * 8:(k + 1) * 8],
                                start=(k == 0), stop=(k == NK - 1))
            mm_pro = self._mm(last)
            s.wait(self.mm, mm_pro)
            s.activation(M2C, self.ps[4][0:8, 0:1], AF.Copy)
            sc_m2 = s.tok
            v.wait(self.mm, mm_pro)
            v.tensor_tensor(NEGCT, self.ps[5][0:64, 0:8], self.ESUM, OP.mult)
            v.tensor_reduce(NEGC, NEGCT, mybir.AxisListType.X, OP.add)
            v.tensor_single_scalar(NEGC, NEGC, -1.0, OP.mult)
            v.tensor_tensor(NEGCT, self.ps[6][0:64, 0:8], self.ESUM, OP.mult)
            v.tensor_reduce(NEGC2, NEGCT, mybir.AxisListType.X, OP.add)
            v.tensor_single_scalar(NEGC2, NEGC2, -1.0, OP.mult)
            v.tensor_tensor(T81, SLOP, CENT, OP.mult)
            v.tensor_single_scalar(GBIAS, T81, -1.0, OP.mult)
            ev_pro = v.tok
            t.wait_ge(self.ev.h, ev_pro)
            t.wait_ge(self.sc.h, sc_m2)
            s.wait(self.ev, ev_pro)

        st = [dict() for _ in range(NTC)]
        dg = {"ev0": 0, "ev1": 0, "st0": 0, "st1": 0, "st2": 0, "st3": 0}

        def P1(tc):
            p = tc % 2
            cs = slice(tc * TC, (tc + 1) * TC)
            pA, pN, pS = self.ps[0 + p], self.ps[4 + p], self.ps[6 + p]
            o = st[tc]
            if tc >= 2:
                po = st[tc - 2]
                t.wait_ge(self.ev.h, po["ev_mx"])
                s.wait(self.ev, po["ev_fg"])
            for k in range(NK):
                last = t.matmul(pA[0:64, :], SELW[:, k * 64:(k + 1) * 64],
                                XBF[:, k * T + tc * TC: k * T + (tc + 1) * TC],
                                start=(k == 0), stop=(k == NK - 1))
            o["mm_a"] = self._mm(last)
            s.wait(self.mm, o["mm_a"])
            s.activation(V2F[p], pA[0:64, :], AF.Square, bias=NEGC)
            sc_v2 = s.tok
            t.wait_ge(self.sc.h, sc_v2)
            if tc < 2:
                t.wait_ge(self.ev.h, self.ln_ev_stats)
            if tc >= 2:
                t.wait_ge(self.ev.h, st[tc - 2]["ev_rows"])
            o["mm_num"] = self._mm(t.matmul(pN[0:8, :], self.ESUM, V2F[p],
                                            start=True, stop=True))
            for k in range(NK):
                last = t.matmul(pA[0:8, :], MUB[:, k * 8:(k + 1) * 8],
                                XBF[:, k * T + tc * TC: k * T + (tc + 1) * TC],
                                start=(k == 0), stop=(k == NK - 1))
            o["mm_q"] = self._mm(last)
            s.wait(self.mm, o["mm_q"])
            s.activation(QSF[p], pA[0:8, :], AF.Copy)
            o["sc_q"] = s.tok
            t.wait_ge(self.sc.h, o["sc_q"])
            for k in range(NK):
                last = t.matmul(pA[0:64, :], UPD[:, k * 64:(k + 1) * 64],
                                XBF[:, k * T + tc * TC: k * T + (tc + 1) * TC],
                                start=(k == 0), stop=(k == NK - 1))
            o["mm_u"] = self._mm(last)
            v.wait(self.mm, o["mm_u"])
            v.tensor_scalar(MODX[p], pA[0:64, :], NEGC2, None, OP.add)
            o["ev_mx"] = v.tok
            v.wait(self.sc, o["sc_q"])
            v.scalar_tensor_tensor(DEN[p], QSF[p], -2.0,
                                   self.BIG_F[0:8, 11008 + tc * TC:
                                              11008 + (tc + 1) * TC],
                                   OP.mult, OP.add)
            v.tensor_scalar(DEN[p], DEN[p], M2C, None, OP.add)
            o["ev_den"] = v.tok
            s.wait(self.ev, o["ev_den"])
            s.activation(DEN[p], DEN[p], AF.Ln)
            s.activation(DEN[p], DEN[p], AF.Exp, scale=-1.0)
            sc_den = s.tok
            v.wait(self.sc, sc_den)
            v.wait(self.mm, o["mm_num"])
            v.tensor_tensor(QSF[p], pN[0:8, :], DEN[p], OP.mult)
            o["ev_sco"] = v.tok
            s.wait(self.ev, o["ev_sco"])
            s.activation(GF[p], QSF[p], AF.Sigmoid, scale=SLOP, bias=GBIAS)
            o["sc_g"] = s.tok

        def P2(tc):
            p = tc % 2
            cs = slice(tc * TC, (tc + 1) * TC)
            pN, pS = self.ps[4 + p], self.ps[6 + p]
            o = st[tc]
            t.wait_ge(self.sc.h, o["sc_g"])
            if tc >= 2:
                t.wait_ge(self.ev.h, st[tc - 2]["ep_ev"])
            for i in range(4):
                last = t.transpose(pS[:, i * 8:(i + 1) * 8],
                                   GF[p][:, i * 128:(i + 1) * 128],
                                   self.ID128[0:8, 0:8])
            mm_tr = self._mm(last)
            v.wait(self.mm, mm_tr)
            if tc >= 2:
                v.wait(self.mm, st[tc - 2]["mm_tb"])
            for i in range(4):
                v.tensor_reduce(MX[p][:, i:i + 1], pS[:, i * 8:(i + 1) * 8],
                                mybir.AxisListType.X, OP.max)
            for i in range(4):
                v.tensor_scalar(IMW[p][:, i * 8:(i + 1) * 8],
                                pS[:, i * 8:(i + 1) * 8],
                                MX[p][:, i:i + 1], None, OP.is_equal)
            for i in range(4):
                v.tensor_tensor_scan(PSW[p][:, i * 8:(i + 1) * 8],
                                     self.ONE8[:], IMW[p][:, i * 8:(i + 1) * 8],
                                     0.0, OP.mult, OP.add)
            v.tensor_single_scalar(PSW[p], PSW[p], 1.0, OP.is_equal)
            v.tensor_tensor(FMW[p], IMW[p], PSW[p], OP.mult)
            o["ev_fm"] = v.tok
            t.wait_ge(self.ev.h, o["ev_fm"])
            for i in range(4):
                last = t.transpose(pS[0:8, i * 128:(i + 1) * 128],
                                   FMW[p][:, i * 8:(i + 1) * 8], self.ID128[:])
            o["mm_tb"] = self._mm(last)
            s.wait(self.mm, o["mm_tb"])
            s.activation(FMS[p], pS[0:8, :], AF.Copy)
            sc_fm = s.tok
            v.wait(self.sc, sc_fm)
            if tc >= 2:
                v.wait(self.mm, st[tc - 2]["ep_mm"])
            v.scalar_tensor_tensor(FGF[p], GF[p], 0.5, FMS[p], OP.mult, OP.mult)
            o["ev_fg"] = v.tok
            s.wait(self.ev, o["ev_fg"])
            if tc >= 2:
                s.wait(self.mm, st[tc - 2]["ep_mm"])
            s.activation(FGB[p], FGF[p], AF.Copy)
            sc_fgb = s.tok
            t.wait_ge(self.ev.h, o["ev_sco"])
            t.wait_ge(self.ev.h, o["ev_fg"])
            mm_s = self._mm(t.matmul(pN[0:1, :], self.ONEF[0:8, :], FGF[p],
                                     start=True, stop=True))
            t.wait_ge(self.sc.h, sc_fm)
            mm_e2 = self._mm(t.matmul(pS[0:64, :], self.E2F, FGF[p],
                                      start=True, stop=True))
            v.wait(self.mm, mm_e2)
            v.tensor_tensor(MMX[p], MODX[p], pS[0:64, :], OP.mult)
            o["ev_mmx"] = v.tok
            v.wait(self.mm, mm_s)
            v.tensor_scalar(RC[p], pN[0:1, :], -1.0, 0.5, OP.mult, OP.add)
            v.tensor_tensor(RC[p], self.RROW[:, cs], RC[p], OP.mult)
            v.scalar_tensor_tensor(MCB[p], self.MROW[:, cs], -1.0, RC[p],
                                   OP.mult, OP.mult)
            v.tensor_single_scalar(RC[p], RC[p], 1.0, OP.add)
            o["ev_rows"] = v.tok
            t.wait_ge(self.ev.h, o["ev_mmx"])
            t.wait_ge(self.ev.h, o["ev_rows"])
            mm_bc = self._mm(t.matmul(pS[:], self.ONE1[:], RC[p],
                                      start=True, stop=True))
            t.wait_ge(self.sc.h, sc_fgb)
            if tc == 0:
                self.nc.scalar.wait_ge(self.ev.h, self.ln_done)
                self.nc.gpsimd.wait_ge(self.ev.h, self.ln_done)
            for dblk in range(NK):
                bank = self.ps[2 + (dblk % 2)]
                rs = dblk % 4
                if dg["ev" + str(dblk % 2)]:
                    t.wait_ge(self.ev.h, dg["ev" + str(dblk % 2)])
                t.matmul(bank[:], DGN[:, dblk * 128:(dblk + 1) * 128], MMX[p],
                         start=True, stop=False)
                t.matmul(bank[:], BIA[:, dblk * 128:(dblk + 1) * 128], FGB[p],
                         start=False, stop=False)
                mm_d = self._mm(t.matmul(bank[:], ONE1B, MCB[p],
                                         start=False, stop=True))
                rslot = self.RESG[:, rs * TC:(rs + 1) * TC]
                ldq = self.nc.gpsimd if rs % 2 == 1 else self.nc.scalar
                lsem = self.rslg[rs // 2] if rs % 2 == 1 else self.rsl[rs]
                if dg["st" + str(rs)]:
                    ldq.wait_ge(self.ssl[rs].h, dg["st" + str(rs)])
                rtok = self._dma(ldq, lsem, rslot,
                                 res_src[dblk * 128:(dblk + 1) * 128, cs])
                v.wait(self.mm, mm_d)
                v.wait(lsem, rtok)
                v.tensor_tensor(rslot, rslot, pS[:], OP.mult)
                v.tensor_tensor(rslot, rslot, bank[:], OP.add)
                dg["ev" + str(dblk % 2)] = v.tok
                q.wait_ge(self.ev.h, dg["ev" + str(dblk % 2)])
                dg["st" + str(rs)] = self._dma(
                    q, self.ssl[rs],
                    res_dst[dblk * 128:(dblk + 1) * 128, cs], rslot)
            o["ep_ev"] = v.tok
            o["ep_mm"] = self.mm.n

        P1(0)
        P1(1)
        P2(0)
        P1(2)
        P2(1)
        P1(3)
        P2(2)
        P2(3)

    def emit_l1attn(self, res_src, res_dst, wx, wz, wp, wo, wdt, convw, convb,
                    alog, dval, bdt):
        t, v, s, g, q = self.t, self.v, self.s, self.g, self.q
        self.emit_ln(res_src)
        XBF, WBUF, CON, BF = self.XBF, self.WBUF, self.CON, self.BIG_F
        XF = self.XBF_F
        ZS = self.BIGB[:, 16384:24576]
        XAB = self.BIGB[:, 24576:32768]
        WDTB = WBUF[:, 4608:4640]
        ACOL = CON[:, 16:20]
        DCOL2 = CON[:, 20:24]
        CVB = CON[:, 24:28]
        CVW = CON[:, 28:44]
        BDTC = CON[0:32, 44:45]
        ATMP = CON[:, 60:64]
        DELT = self.SM[0:32, 0:512]
        SP1 = self.SM[0:32, 512:1024]
        HCAR = self.TMPF[:, 0:4]
        for dst, src in ((CVW, convw), (CVB, convb), (ATMP, alog),
                         (DCOL2, dval), (BDTC, bdt)):
            self._dma(q, self.xq, dst, src)
        xq_tok = self.xq.n
        s.wait(self.xq, xq_tok)
        s.activation(ACOL, ATMP, AF.Exp)
        s.mul(ACOL, ACOL, -1.0)
        sc_acol = s.tok
        sc_tok = [0] * 8
        mm_tok = [0] * 8
        for ob in range(8):
            w_src = wx[ob] if ob < 4 else wz[ob - 4]
            slot = ob % 2
            wsl = WBUF[:, slot * 2048:(slot + 1) * 2048]
            if ob >= 2:
                g.wait_ge(self.mm.h, mm_tok[ob - 2])
            tok = self._dma(g, self.wsl[slot], wsl,
                            w_src.rearrange("p k c -> p (k c)"))
            banks = self.ps[0:4] if ob % 2 == 0 else self.ps[4:8]
            t.wait_ge(self.wsl[slot].h, tok)
            if ob >= 2:
                t.wait_ge(self.sc.h, sc_tok[ob - 2])
            if ob == 1:
                t.wait_ge(self.ev.h, self.ln_ev_stats)
            for tc in range(NTC):
                if ob < 2:
                    t.wait_ge(self.ev.h, self.ln_ev_tc[tc])
                for k in range(NK):
                    last = t.matmul(banks[tc][:], wsl[:, k * 128:(k + 1) * 128],
                                    XBF[:, k * T + tc * TC: k * T + (tc + 1) * TC],
                                    start=(k == 0), stop=(k == NK - 1))
            mm_tok[ob] = self._mm(last)
            s.wait(self.mm, mm_tok[ob])
            for tc in range(NTC):
                if ob < 4:
                    s.activation(BF[:, ob * 2048 + tc * TC: ob * 2048 + (tc + 1) * TC],
                                 banks[tc][:], AF.Copy)
                else:
                    s.activation(ZS[:, (ob - 4) * 2048 + tc * TC:
                                    (ob - 4) * 2048 + (tc + 1) * TC],
                                 banks[tc][:], AF.Silu)
            sc_tok[ob] = s.tok
        xz_done_mm = self.mm.n
        acc = self.TMPF[:, 0:2048]
        conv_sc = [0] * 4
        for ct4 in range(4):
            XP = BF[:, ct4 * 2048:(ct4 + 1) * 2048]
            v.wait(self.sc, sc_tok[ct4])
            if ct4 > 0:
                v.wait(self.sc, conv_sc[ct4 - 1])
            v.tensor_scalar(acc, XP, CVW[:, ct4 * 4 + 3: ct4 * 4 + 4], None, OP.mult)
            v.scalar_tensor_tensor(acc[:, 1:2048], XP[:, 0:2047],
                                   CVW[:, ct4 * 4 + 2: ct4 * 4 + 3],
                                   acc[:, 1:2048], OP.mult, OP.add)
            v.scalar_tensor_tensor(acc[:, 2:2048], XP[:, 0:2046],
                                   CVW[:, ct4 * 4 + 1: ct4 * 4 + 2],
                                   acc[:, 2:2048], OP.mult, OP.add)
            v.scalar_tensor_tensor(acc[:, 3:2048], XP[:, 0:2045],
                                   CVW[:, ct4 * 4 + 0: ct4 * 4 + 1],
                                   acc[:, 3:2048], OP.mult, OP.add)
            ev_conv = v.tok
            s.wait(self.ev, ev_conv)
            s.activation(XAB[:, ct4 * 2048:(ct4 + 1) * 2048], acc, AF.Silu,
                         bias=CVB[:, ct4:ct4 + 1])
            s.activation(XP, acc, AF.Silu, bias=CVB[:, ct4:ct4 + 1])
            conv_sc[ct4] = s.tok
        SMB = self.SM.ap().bitcast(BF16)
        wo_dst = ([self.HEV[:, i * 512:(i + 1) * 512] for i in range(8)] +
                  [WBUF[:, 4640 + i * 512: 4640 + (i + 1) * 512]
                   for i in range(6)] +
                  [SMB[:, i * 512:(i + 1) * 512] for i in range(2)])
        g.wait_ge(self.mm.h, xz_done_mm)
        for ob in range(9):
            self._dma(g, self.wgrp, WBUF[:, ob * 512:(ob + 1) * 512],
                      wp[ob].rearrange("p k c -> p (k c)"))
        self._dma(g, self.wgrp, WDTB, wdt)
        g.wait_ge(self.mm.h, self.ln_mm_stats)
        for dblk in range(NK):
            self._dma(g, self.wgrp, wo_dst[dblk],
                      wo[dblk].rearrange("p k c -> p (k c)"))
        wp_tok = self.wgrp.n
        t.wait_ge(self.wgrp.h, wp_tok)
        t.wait_ge(self.sc.h, conv_sc[3])
        t.wait_ge(self.sc.h, sc_acol)
        YBF = self.XBF[:, 24576:26624]
        DTF = [self.XBF[:, 26624:27136], self.XBF[:, 27136:27648],
               self.XBF[:, 27648:28160]]
        DELT = [self.TMPF[0:32, 4:516], self.TMPF[0:32, 516:1028],
                self.TMPF[0:32, 1028:1540]]
        SP1 = self.RESG[0:32, 0:512]
        PAR_B = [XF[:, 0:2048], XF[:, 2048:4096], XF[:, 4096:6144]]
        PAR_C = [XF[:, 6144:8192], XF[:, 8192:10240], XF[:, 10240:12288]]
        ABAR = XF[:, 12288:14336]
        HST = XF[:, 14336:16384]
        st = [dict() for _ in range(NTC)]
        par_bank_sc = [0, 0, 0]
        dbc_ev = [0, 0]
        wo_mm = [0, 0]
        wo_st3 = [0, 0, 0]
        wo_ev = [0, 0]

        def P1(u):
            pu = u % 3
            o = st[u]
            if u >= 3:
                s.wait(self.ev, st[u - 3]["ev_y"])
            for ob in range(9):
                bank = self.ps[ob % 3]
                if par_bank_sc[ob % 3]:
                    t.wait_ge(self.sc.h, par_bank_sc[ob % 3])
                for kc in range(4):
                    last = t.matmul(bank[0:128, :],
                                    WBUF[:, ob * 512 + kc * 128:
                                         ob * 512 + (kc + 1) * 128],
                                    XAB[:, kc * 2048 + u * TC:
                                        kc * 2048 + (u + 1) * TC],
                                    start=(kc == 0), stop=(kc == 3))
                mm_par = self._mm(last)
                s.wait(self.mm, mm_par)
                if ob == 0:
                    s.activation(DTF[pu], bank[0:128, :], AF.Copy)
                elif ob <= 4:
                    s.activation(PAR_B[pu][:, (ob - 1) * TC:ob * TC],
                                 bank[0:128, :], AF.Copy)
                else:
                    s.activation(PAR_C[pu][:, (ob - 5) * TC:(ob - 4) * TC],
                                 bank[0:128, :], AF.Copy)
                par_bank_sc[ob % 3] = s.tok
                if ob == 0:
                    o["sc_dtf"] = s.tok
            o["sc_par"] = s.tok
            t.wait_ge(self.sc.h, o["sc_dtf"])
            if u >= 1:
                t.wait_ge(self.ev.h, st[u - 1]["ev_d0"])
            o["mm_dt"] = self._mm(t.matmul(self.ps[7][0:32, :], WDTB, DTF[pu],
                                           start=True, stop=True))
            v.wait(self.mm, o["mm_dt"])
            if u >= 3:
                v.wait(self.mm, st[u - 3]["mm_dbc"])
            v.tensor_scalar(DELT[pu], self.ps[7][0:32, :], BDTC, None, OP.add)
            o["ev_d0"] = v.tok
            s.wait(self.ev, o["ev_d0"])
            if u >= 1:
                s.wait(self.ev, st[u - 1]["ev_delt"])
            s.activation(SP1, DELT[pu], AF.Abs)
            s.activation(SP1, SP1, AF.Exp, scale=-1.0)
            s.activation(SP1, SP1, AF.Ln, bias=self.ONEF[0:32, :])
            sc_sp = s.tok
            v.wait(self.sc, sc_sp)
            v.tensor_scalar(DELT[pu], DELT[pu], 0.0, None, OP.max)
            v.tensor_tensor(DELT[pu], DELT[pu], SP1, OP.add)
            o["ev_delt"] = v.tok

        def P2(u):
            pu = u % 3
            ccs = slice(u * TC, (u + 1) * TC)
            o = st[u]
            t.wait_ge(self.ev.h, o["ev_delt"])
            if u >= 1:
                v.wait(self.mm, st[u - 1]["wo_mm"])
                s.wait(self.ev, st[u - 1]["ev_y"])
            v.wait(self.sc, o["sc_par"])
            mm_dbc = 0
            for ct4 in range(4):
                dbc = self.ps[3 + (ct4 % 2)]
                if dbc_ev[ct4 % 2]:
                    t.wait_ge(self.ev.h, dbc_ev[ct4 % 2])
                mm_dbc = self._mm(t.matmul(
                    dbc[:], self.EHN[:, ct4 * 128:(ct4 + 1) * 128], DELT[pu],
                    start=True, stop=True))
                s.wait(self.mm, mm_dbc)
                s.activation(ABAR[:, ct4 * TC:(ct4 + 1) * TC], dbc[:], AF.Exp,
                             scale=ACOL[:, ct4:ct4 + 1])
                sc_ab = s.tok
                pb = PAR_B[pu][:, ct4 * TC:(ct4 + 1) * TC]
                v.tensor_tensor(pb, pb, BF[:, ct4 * 2048 + u * TC:
                                           ct4 * 2048 + (u + 1) * TC], OP.mult)
                v.wait(self.sc, sc_ab)
                v.tensor_tensor(pb, pb, dbc[:], OP.mult)
                dbc_ev[ct4 % 2] = v.tok
                ab = ABAR[:, ct4 * TC:(ct4 + 1) * TC]
                hst = HST[:, ct4 * TC:(ct4 + 1) * TC]
                init = 0.0 if u == 0 else HCAR[:, ct4:ct4 + 1]
                v.tensor_tensor_scan(hst, ab, pb, init, OP.mult, OP.add)
                v.tensor_copy(HCAR[:, ct4:ct4 + 1], hst[:, TC - 1:TC])
                pc = PAR_C[pu][:, ct4 * TC:(ct4 + 1) * TC]
                v.tensor_tensor(pc, pc, hst, OP.mult)
                v.scalar_tensor_tensor(pc, BF[:, ct4 * 2048 + u * TC:
                                              ct4 * 2048 + (u + 1) * TC],
                                       DCOL2[:, ct4:ct4 + 1], pc, OP.mult,
                                       OP.add)
                v.tensor_tensor(YBF[:, ct4 * TC:(ct4 + 1) * TC], pc,
                                ZS[:, ct4 * 2048 + u * TC:
                                   ct4 * 2048 + (u + 1) * TC], OP.mult)
            o["ev_y"] = v.tok
            o["mm_dbc"] = mm_dbc
            for dblk in range(NK):
                slot = dblk % 2
                rs = dblk % 3
                wsl = wo_dst[dblk]
                bank = self.ps[5 + slot]
                t.wait_ge(self.ev.h, o["ev_y"])
                if wo_ev[slot]:
                    t.wait_ge(self.ev.h, wo_ev[slot])
                for kc in range(4):
                    last = t.matmul(bank[:], wsl[:, kc * 128:(kc + 1) * 128],
                                    YBF[:, kc * TC:(kc + 1) * TC],
                                    start=(kc == 0), stop=(kc == 3))
                wo_mm[slot] = self._mm(last)
                rslot = self.RESG[:, 512 + rs * TC: 512 + (rs + 1) * TC]
                ldq = self.nc.gpsimd if rs == 1 else self.nc.scalar
                lsem = self.rslg[0] if rs == 1 else self.rsl[rs]
                if wo_st3[rs]:
                    ldq.wait_ge(self.ssl[rs].h, wo_st3[rs])
                rtok = self._dma(ldq, lsem, rslot,
                                 res_src[dblk * 128:(dblk + 1) * 128, ccs])
                v.wait(self.mm, wo_mm[slot])
                v.wait(lsem, rtok)
                v.tensor_tensor(rslot, bank[:], rslot, OP.add)
                wo_ev[slot] = v.tok
                q.wait_ge(self.ev.h, wo_ev[slot])
                wo_st3[rs] = self._dma(
                    q, self.ssl[rs],
                    res_dst[dblk * 128:(dblk + 1) * 128, ccs], rslot)
            o["wo_mm"] = wo_mm[slot]

        P1(0)
        P1(1)
        P1(2)
        P2(0)
        P1(3)
        P2(1)
        P2(2)
        P2(3)

    def barrier_all(self):
        sems = ([self.xin, self.wgrp, self.hld, self.xq, self.mm, self.ev,
                 self.sc, self.gp] + self.xint + self.wsl + self.rsl +
                self.rslg + self.ssl + self.hsl)
        for eng in (self.t, self.q, self.nc.vector, self.nc.scalar, self.nc.gpsimd):
            for sem in sems:
                if sem.n > 0:
                    eng.wait_ge(sem.h, sem.n)


def host_layout_l0ffn(w_gate_up, w_down):
    wgu = np.ascontiguousarray(
        w_gate_up.T.reshape(16, 128, 128, 128).transpose(2, 1, 0, 3))
    wdn = np.ascontiguousarray(
        w_down.T.reshape(64, 128, 16, 128).transpose(2, 1, 0, 3))
    return wgu, wdn


def host_layout_l0attn(B, C, Dv):
    bt = np.ascontiguousarray(B.T.reshape(16, 128, 8, 128).transpose(2, 1, 0, 3))
    ct = np.ascontiguousarray(C.T.reshape(8, 128, 16, 128).transpose(2, 1, 0, 3))
    dcol = np.ascontiguousarray(Dv.reshape(16, 128).T)
    return bt, ct, dcol


def host_layout_l1ffn(select_weight, select_mean_diff, imp_center, imp_slope,
                      lora_update, lora_degen, bias_w, debias_w):
    selw = np.ascontiguousarray(
        select_weight.transpose(1, 0, 2).reshape(16, 128, 64).transpose(1, 0, 2))
    mu = np.ascontiguousarray(
        select_mean_diff.T.reshape(16, 128, 8).transpose(1, 0, 2))
    upd = np.ascontiguousarray(
        lora_update.transpose(1, 0, 2).reshape(16, 128, 64).transpose(1, 0, 2))
    dgn = np.ascontiguousarray(lora_degen.transpose(0, 2, 1).reshape(64, 2048))
    bias = np.ascontiguousarray(bias_w)
    debias = np.ascontiguousarray(
        debias_w.T.reshape(16, 128, 8).transpose(1, 0, 2))
    censlo = np.ascontiguousarray(np.stack([imp_center, imp_slope], axis=1))
    return selw, mu, upd, dgn, bias, debias, censlo


def host_layout_l1attn(W_in_x, W_in_z, conv_w, conv_b, W_param, W_dt, b_dt,
                       A_log, Dv, W_out):
    wx = np.ascontiguousarray(
        W_in_x.T.reshape(16, 128, 4, 128).transpose(2, 1, 0, 3))
    wz = np.ascontiguousarray(
        W_in_z.T.reshape(16, 128, 4, 128).transpose(2, 1, 0, 3))
    wp = np.ascontiguousarray(
        W_param.T.reshape(4, 128, 9, 128).transpose(2, 1, 0, 3))
    wo = np.ascontiguousarray(
        W_out.T.reshape(4, 128, 16, 128).transpose(2, 1, 0, 3))
    wdt = np.ascontiguousarray(W_dt.T)
    convw = np.ascontiguousarray(
        conv_w.reshape(512, 4).reshape(4, 128, 4).transpose(1, 0, 2).reshape(128, 16))
    convb = np.ascontiguousarray(conv_b.reshape(4, 128).T)
    alog = np.ascontiguousarray(A_log.reshape(512).reshape(4, 128).T)
    dval = np.ascontiguousarray(Dv.reshape(4, 128).T)
    bdt = np.ascontiguousarray(b_dt.reshape(32, 1))
    return wx, wz, wp, wo, wdt, convw, convb, alog, dval, bdt


def build_l0ffn_only():
    nc = bass.Bass("TRN2", target_bir_lowering=False)
    res_in = nc.dram_tensor("res_in", [D, T], F32, kind="ExternalInput").ap()
    wgu = nc.dram_tensor("wgu", [128, 128, 16, 128], F32, kind="ExternalInput").ap()
    wdn = nc.dram_tensor("wdn", [16, 128, 64, 128], F32, kind="ExternalInput").ap()
    res_out = nc.dram_tensor("res_out", [D, T], F32, kind="ExternalOutput").ap()
    h_scr = nc.dram_tensor("h_scr", [64, 128, T], BF16, kind="Internal").ap()
    p = P(nc)
    p.emit_consts()
    p.emit_l0ffn(res_in, res_out, wgu, wdn, h_scr)
    p.barrier_all()
    return nc


def build_l0attn_only():
    nc = bass.Bass("TRN2", target_bir_lowering=False)
    res_in = nc.dram_tensor("res_in", [D, T], F32, kind="ExternalInput").ap()
    bt = nc.dram_tensor("bt", [8, 128, 16, 128], F32, kind="ExternalInput").ap()
    ct = nc.dram_tensor("ct", [16, 128, 8, 128], F32, kind="ExternalInput").ap()
    dcol = nc.dram_tensor("dcol", [128, 16], F32, kind="ExternalInput").ap()
    res_out = nc.dram_tensor("res_out", [D, T], F32, kind="ExternalOutput").ap()
    p = P(nc)
    p.emit_consts()
    p.emit_l0attn(res_in, res_out, bt, ct, dcol)
    p.barrier_all()
    return nc


def build_l1ffn_only():
    nc = bass.Bass("TRN2", target_bir_lowering=False)
    res_in = nc.dram_tensor("res_in", [D, T], F32, kind="ExternalInput").ap()
    selw = nc.dram_tensor("selw", [128, 16, 64], F32, kind="ExternalInput").ap()
    mu = nc.dram_tensor("mu", [128, 16, 8], F32, kind="ExternalInput").ap()
    upd = nc.dram_tensor("upd", [128, 16, 64], F32, kind="ExternalInput").ap()
    dgn = nc.dram_tensor("dgn", [64, 2048], F32, kind="ExternalInput").ap()
    bias = nc.dram_tensor("bias", [8, 2048], F32, kind="ExternalInput").ap()
    debias = nc.dram_tensor("debias", [128, 16, 8], F32, kind="ExternalInput").ap()
    censlo = nc.dram_tensor("censlo", [8, 2], F32, kind="ExternalInput").ap()
    res_out = nc.dram_tensor("res_out", [D, T], F32, kind="ExternalOutput").ap()
    p = P(nc)
    p.emit_consts()
    p.emit_l1ffn(res_in, res_out, selw, mu, upd, dgn, bias, debias, censlo, first=True)
    p.barrier_all()
    return nc


def build_l1attn_only():
    nc = bass.Bass("TRN2", target_bir_lowering=False)
    res_in = nc.dram_tensor("res_in", [D, T], F32, kind="ExternalInput").ap()
    wx = nc.dram_tensor("wx", [4, 128, 16, 128], F32, kind="ExternalInput").ap()
    wz = nc.dram_tensor("wz", [4, 128, 16, 128], F32, kind="ExternalInput").ap()
    wp = nc.dram_tensor("wp", [9, 128, 4, 128], F32, kind="ExternalInput").ap()
    wo = nc.dram_tensor("wo", [16, 128, 4, 128], F32, kind="ExternalInput").ap()
    wdt = nc.dram_tensor("wdt", [128, 32], F32, kind="ExternalInput").ap()
    convw = nc.dram_tensor("convw", [128, 16], F32, kind="ExternalInput").ap()
    convb = nc.dram_tensor("convb", [128, 4], F32, kind="ExternalInput").ap()
    alog = nc.dram_tensor("alog", [128, 4], F32, kind="ExternalInput").ap()
    dval = nc.dram_tensor("dval", [128, 4], F32, kind="ExternalInput").ap()
    bdt = nc.dram_tensor("bdt", [32, 1], F32, kind="ExternalInput").ap()
    res_out = nc.dram_tensor("res_out", [D, T], F32, kind="ExternalOutput").ap()
    p = P(nc)
    p.emit_consts()
    p.emit_l1attn(res_in, res_out, wx, wz, wp, wo, wdt, convw, convb,
                  alog, dval, bdt)
    p.barrier_all()
    return nc


_PROG_CACHE = {}


def _prog(name, builder):
    if name not in _PROG_CACHE:
        _PROG_CACHE[name] = builder()
    return _PROG_CACHE[name]


def kernel_unfused(**inputs):
    wgu, wdn = host_layout_l0ffn(np.asarray(inputs["L0_ffn_w_gate_up"]),
                                 np.asarray(inputs["L0_ffn_w_down"]))
    bt, ct, dcol = host_layout_l0attn(np.asarray(inputs["L0_attn_B"]),
                                      np.asarray(inputs["L0_attn_C"]),
                                      np.asarray(inputs["L0_attn_D"]))
    selw, mu, upd, dgn, bias, debias, censlo = host_layout_l1ffn(
        np.asarray(inputs["L1_ffn_select_weight"]),
        np.asarray(inputs["L1_ffn_select_mean_diff"]),
        np.asarray(inputs["L1_ffn_imp_center"]),
        np.asarray(inputs["L1_ffn_imp_slope"]),
        np.asarray(inputs["L1_ffn_lora_update"]),
        np.asarray(inputs["L1_ffn_lora_degen"]),
        np.asarray(inputs["L1_ffn_bias_w"]),
        np.asarray(inputs["L1_ffn_debias_w"]))
    wx, wz, wp, wo, wdt, convw, convb, alog, dval, bdt = host_layout_l1attn(
        np.asarray(inputs["L1_attn_W_in_x"]), np.asarray(inputs["L1_attn_W_in_z"]),
        np.asarray(inputs["L1_attn_conv_w"]), np.asarray(inputs["L1_attn_conv_b"]),
        np.asarray(inputs["L1_attn_W_param"]), np.asarray(inputs["L1_attn_W_dt"]),
        np.asarray(inputs["L1_attn_b_dt"]), np.asarray(inputs["L1_attn_A_log"]),
        np.asarray(inputs["L1_attn_D"]), np.asarray(inputs["L1_attn_W_out"]))
    x = np.asarray(inputs["L0_attn_x"])
    res = [np.ascontiguousarray(x[b].T) for b in range(8)]

    launches = [
        ("l0ffn", build_l0ffn_only, {"wgu": wgu, "wdn": wdn}),
        ("l0attn", build_l0attn_only, {"bt": bt, "ct": ct, "dcol": dcol}),
        ("l0ffn", build_l0ffn_only, {"wgu": wgu, "wdn": wdn}),
        ("l1ffn", build_l1ffn_only, {"selw": selw, "mu": mu, "upd": upd,
                                     "dgn": dgn, "bias": bias, "debias": debias,
                                     "censlo": censlo}),
        ("l1attn", build_l1attn_only, {"wx": wx, "wz": wz, "wp": wp, "wo": wo,
                                       "wdt": wdt, "convw": convw, "convb": convb,
                                       "alog": alog, "dval": dval, "bdt": bdt}),
        ("l1ffn", build_l1ffn_only, {"selw": selw, "mu": mu, "upd": upd,
                                     "dgn": dgn, "bias": bias, "debias": debias,
                                     "censlo": censlo}),
    ]
    for name, builder, wmap in launches:
        nc = _prog(name, builder)
        in_maps = [dict(wmap, res_in=res[b]) for b in range(8)]
        out = run_bass_kernel_spmd(nc, in_maps, core_ids=list(range(8)))
        res = [out.results[b]["res_out"] for b in range(8)]
    return np.stack([r.T for r in res]).astype(np.float32)


def build_fused():
    nc = bass.Bass("TRN2", target_bir_lowering=False)
    DT = nc.dram_tensor
    res_in = DT("res_in", [D, T], F32, kind="ExternalInput").ap()
    wgu = DT("wgu", [128, 128, 16, 128], F32, kind="ExternalInput").ap()
    wdn = DT("wdn", [16, 128, 64, 128], F32, kind="ExternalInput").ap()
    bt = DT("bt", [8, 128, 16, 128], F32, kind="ExternalInput").ap()
    ct = DT("ct", [16, 128, 8, 128], F32, kind="ExternalInput").ap()
    dcol = DT("dcol", [128, 16], F32, kind="ExternalInput").ap()
    selw = DT("selw", [128, 16, 64], F32, kind="ExternalInput").ap()
    mu = DT("mu", [128, 16, 8], F32, kind="ExternalInput").ap()
    upd = DT("upd", [128, 16, 64], F32, kind="ExternalInput").ap()
    dgn = DT("dgn", [64, 2048], F32, kind="ExternalInput").ap()
    bias = DT("bias", [8, 2048], F32, kind="ExternalInput").ap()
    debias = DT("debias", [128, 16, 8], F32, kind="ExternalInput").ap()
    censlo = DT("censlo", [8, 2], F32, kind="ExternalInput").ap()
    wx = DT("wx", [4, 128, 16, 128], F32, kind="ExternalInput").ap()
    wz = DT("wz", [4, 128, 16, 128], F32, kind="ExternalInput").ap()
    wp = DT("wp", [9, 128, 4, 128], F32, kind="ExternalInput").ap()
    wo = DT("wo", [16, 128, 4, 128], F32, kind="ExternalInput").ap()
    wdt = DT("wdt", [128, 32], F32, kind="ExternalInput").ap()
    convw = DT("convw", [128, 16], F32, kind="ExternalInput").ap()
    convb = DT("convb", [128, 4], F32, kind="ExternalInput").ap()
    alog = DT("alog", [128, 4], F32, kind="ExternalInput").ap()
    dval = DT("dval", [128, 4], F32, kind="ExternalInput").ap()
    bdt = DT("bdt", [32, 1], F32, kind="ExternalInput").ap()
    res_out = DT("res_out", [D, T], F32, kind="ExternalOutput").ap()
    ra = DT("res_a", [D, T], F32, kind="Internal").ap()
    rb = DT("res_b", [D, T], F32, kind="Internal").ap()
    h_scr = DT("h_scr", [64, 128, T], BF16, kind="Internal").ap()
    p = P(nc)
    p.emit_consts()
    p.emit_l0ffn(res_in, ra, wgu, wdn, h_scr)
    p.barrier_all()
    p.emit_l0attn(ra, rb, bt, ct, dcol)
    p.barrier_all()
    p.emit_l0ffn(rb, ra, wgu, wdn, h_scr)
    p.barrier_all()
    p.emit_l1ffn(ra, rb, selw, mu, upd, dgn, bias, debias, censlo, first=True)
    p.barrier_all()
    p.emit_l1attn(rb, ra, wx, wz, wp, wo, wdt, convw, convb, alog, dval, bdt)
    p.barrier_all()
    p.emit_l1ffn(ra, res_out, selw, mu, upd, dgn, bias, debias, censlo, first=False)
    p.barrier_all()
    return nc


def _host_weight_maps(inputs):
    wgu, wdn = host_layout_l0ffn(np.asarray(inputs["L0_ffn_w_gate_up"]),
                                 np.asarray(inputs["L0_ffn_w_down"]))
    bt, ct, dcol = host_layout_l0attn(np.asarray(inputs["L0_attn_B"]),
                                      np.asarray(inputs["L0_attn_C"]),
                                      np.asarray(inputs["L0_attn_D"]))
    selw, mu, upd, dgn, bias, debias, censlo = host_layout_l1ffn(
        np.asarray(inputs["L1_ffn_select_weight"]),
        np.asarray(inputs["L1_ffn_select_mean_diff"]),
        np.asarray(inputs["L1_ffn_imp_center"]),
        np.asarray(inputs["L1_ffn_imp_slope"]),
        np.asarray(inputs["L1_ffn_lora_update"]),
        np.asarray(inputs["L1_ffn_lora_degen"]),
        np.asarray(inputs["L1_ffn_bias_w"]),
        np.asarray(inputs["L1_ffn_debias_w"]))
    wx, wz, wp, wo, wdt, convw, convb, alog, dval, bdt = host_layout_l1attn(
        np.asarray(inputs["L1_attn_W_in_x"]), np.asarray(inputs["L1_attn_W_in_z"]),
        np.asarray(inputs["L1_attn_conv_w"]), np.asarray(inputs["L1_attn_conv_b"]),
        np.asarray(inputs["L1_attn_W_param"]), np.asarray(inputs["L1_attn_W_dt"]),
        np.asarray(inputs["L1_attn_b_dt"]), np.asarray(inputs["L1_attn_A_log"]),
        np.asarray(inputs["L1_attn_D"]), np.asarray(inputs["L1_attn_W_out"]))
    return {"wgu": wgu, "wdn": wdn, "bt": bt, "ct": ct, "dcol": dcol,
            "selw": selw, "mu": mu, "upd": upd, "dgn": dgn, "bias": bias,
            "debias": debias, "censlo": censlo, "wx": wx, "wz": wz, "wp": wp,
            "wo": wo, "wdt": wdt, "convw": convw, "convb": convb, "alog": alog,
            "dval": dval, "bdt": bdt}


def kernel(**inputs):
    wmap = _host_weight_maps(inputs)
    x = np.asarray(inputs["L0_attn_x"])
    nc = _prog("fused", build_fused)
    in_maps = [dict(wmap, res_in=np.ascontiguousarray(x[b].T)) for b in range(8)]
    out = run_bass_kernel_spmd(nc, in_maps, core_ids=list(range(8)))
    return np.stack([out.results[b]["res_out"].T for b in range(8)]).astype(np.float32)
